# version 18
# baseline (speedup 1.0000x reference)
import numpy as np
import ml_dtypes
from contextlib import ExitStack

import concourse.bass as bass
import concourse.mybir as mybir
from concourse.bass_utils import run_bass_kernel_spmd

F32 = mybir.dt.float32
BF16 = mybir.dt.bfloat16
I32 = mybir.dt.int32
AF = mybir.ActivationFunctionType
ALU = mybir.AluOpType
AX = mybir.AxisListType
bfdt = ml_dtypes.bfloat16

ROT4 = True
D = 4096
KT = 32
NT = 32
TPC = 2048
TC = 1024
NCORES = 8
INV_D = 1.0 / D
EPS = 1e-5


class Sem:
    def __init__(self, h):
        self.h = h
        self.n = 0

    def inc(self, inst, by=1):
        inst.then_inc(self.h, by)
        self.n += by
        return self.n

    def incd(self, eng, by=1):
        eng.drain()
        eng.sem_inc(self.h, by)
        self.n += by
        return self.n


def build_program():
    nc = bass.Bass("TRN2", target_bir_lowering=False)

    xT_in = nc.dram_tensor("xT", [D, TPC], F32, kind="ExternalInput").ap()
    maskT = nc.dram_tensor("maskT", [D, D], I32, kind="ExternalInput").ap()
    wT = nc.dram_tensor("wT", [NT, 128, KT * 128], F32, kind="ExternalInput").ap()
    baseT = nc.dram_tensor("baseT", [NT, 128, KT * 128], F32, kind="ExternalInput").ap()
    gwT_d = nc.dram_tensor("gwT", [D, 8], F32, kind="ExternalInput").ap()
    acatT_d = nc.dram_tensor("acatT", [D, 128], F32, kind="ExternalInput").ap()
    bcat2_d = nc.dram_tensor("bcat2", [128, D], F32, kind="ExternalInput").ap()
    biasA_d = nc.dram_tensor("biasA", [128, 32], F32, kind="ExternalInput").ap()
    biasB_d = nc.dram_tensor("biasB", [128, 32], F32, kind="ExternalInput").ap()
    outT = nc.dram_tensor("outT", [D, TPC], F32, kind="ExternalOutput").ap()
    scrA = nc.dram_tensor("scrA", [NT, 128, KT * 128], BF16, kind="Internal").ap()
    dscr = nc.dram_tensor("dscr", [16, 1024], F32, kind="Internal").ap()
    dbm = nc.dram_tensor("dbm", [128, 128], BF16, kind="Internal").ap()
    dbw = nc.dram_tensor("dbw", [8, TC], BF16, kind="Internal").ap()
    dbr = nc.dram_tensor("dbr", [2, 512], BF16, kind="Internal").ap()

    ident = nc.inline_tensor(np.eye(128, dtype=bfdt), "ident")
    ones_c = nc.inline_tensor(np.ones((128, 1), dtype=bfdt), "ones_c")
    bpk_np = np.zeros((128, 4), dtype=bfdt)
    for p in range(128):
        bpk_np[p, p // 32] = 1.0
    bpk = nc.inline_tensor(bpk_np, "bpk")

    ctx = ExitStack()
    sb = lambda nm, sh, dt: ctx.enter_context(nc.sbuf_tensor(nm, sh, dt))

    xA = sb("xA", [128, KT * TC], BF16)
    xB = sb("xB", [128, KT * TC], BF16)
    wcol = [sb(f"wcol{i}", [128, KT * 128], BF16) for i in range(2)]
    acat_s = sb("acat_s", [128, KT * 128], BF16)
    bcat_s = sb("bcat_s", [128, D], BF16)
    gw_s = sb("gw_s", [128, KT * 8], BF16)
    idsb = sb("idsb", [128, 128], BF16)
    ones = sb("ones", [128, 1], BF16)
    bpksb = sb("bpksb", [128, 4], BF16)
    biasA = sb("biasA_s", [128, 32], F32)
    biasBh = sb("biasBh", [128, 32], F32)
    biasB_t = sb("biasB_t", [128, 32], F32)
    aux_bf = sb("aux_bf", [128, 5120], BF16)
    rows_f = sb("rows_f", [128, 2048], F32)
    r_b = sb("r_b", [128, TC], F32)
    w2exp = sb("w2exp", [128, TC], BF16)
    rmb = sb("rmb", [128, TC], BF16)
    bsum_pm = sb("bsum_pm", [128, 32], F32)
    wsum_pm = sb("wsum_pm", [128, 32], F32)
    zs = sb("zs", [128, TC], BF16)
    sqb = [sb(f"sqb{i}", [128, TC], BF16) for i in range(2)]
    gstage = sb("gstage", [128, 512], BF16)
    epsc = sb("epsc", [128, 1], F32)
    ostg = [sb(f"ostg{i}", [128, 512], F32) for i in range(2)]
    kexp = sb("kexp", [128, KT * 128], BF16)

    pmain = [ctx.enter_context(nc.psum_tensor(f"pm{i}", [128, 512], F32)) for i in range(4)]
    pstat = ctx.enter_context(nc.psum_tensor("pstat", [128, 512], F32))
    pgate = ctx.enter_context(nc.psum_tensor("pgate", [128, 512], F32))
    plow = [ctx.enter_context(nc.psum_tensor(f"pl{i}", [128, 512], F32)) for i in range(2)]

    names = ["s_wcs", "s_bc", "s_scw", "s_out", "s_ld", "s_ldg",
             "s_mm", "s_ep", "s_sq", "s_ax", "s_wm", "s_ws", "s_m1", "s_m1e", "s_bv"]
    S = {nm: Sem(ctx.enter_context(nc.semaphore(nm))) for nm in names}
    s_wcs, s_bc, s_scw, s_out, s_ld, s_ldg = (S[n] for n in names[:6])
    s_mm, s_ep, s_sq, s_ax, s_wm, s_ws, s_m1, s_m1e, s_bv = (S[n] for n in names[6:])
    s_xq = [Sem(ctx.enter_context(nc.semaphore(f"s_xq{i}"))) for i in range(16)]
    s_wc2 = [Sem(ctx.enter_context(nc.semaphore(f"s_wc{i}"))) for i in range(2)]
    s_mk4 = [Sem(ctx.enter_context(nc.semaphore(f"s_mk{i}"))) for i in range(4)]
    s_cvm = Sem(ctx.enter_context(nc.semaphore("s_cvm")))
    s_wcs2 = [Sem(ctx.enter_context(nc.semaphore(f"s_wcs{i}"))) for i in range(2)]
    s_scw2 = [Sem(ctx.enter_context(nc.semaphore(f"s_scw{i}"))) for i in range(2)]
    s_out2 = [Sem(ctx.enter_context(nc.semaphore(f"s_out{i}"))) for i in range(2)]
    s_kx = Sem(ctx.enter_context(nc.semaphore("s_kx")))
    s_cs2 = [Sem(ctx.enter_context(nc.semaphore(f"s_cs{i}"))) for i in range(2)]
    s_mc = Sem(ctx.enter_context(nc.semaphore("s_mc")))
    xq_ready = [None] * 16
    s_bar = Sem(ctx.enter_context(nc.semaphore("s_bar")))
    bar_k = [0]

    def barrier():
        bar_k[0] += 1
        tgt = 5 * bar_k[0]

        def mk(eng):
            eng.drain()
            eng.sem_inc(s_bar.h, 1)
            eng.wait_ge(s_bar.h, tgt)
        block.tensor(mk)
        block.vector(mk)
        block.scalar(mk)
        block.gpsimd(mk)
        block.sync(mk)
        s_bar.n = tgt

    block = ctx.enter_context(nc.Block())

    mskraw = [kexp[:, i * 1024:(i + 1) * 1024].bitcast(I32) for i in range(3)]
    knb_f = kexp[:, 3072:3584].bitcast(F32)
    knb_b = kexp[:, 3584:3968]
    bm01T = kexp[:, 3968:4096]
    gv = rows_f[:, 512:576]
    wvrep = aux_bf[:, 0:1024]
    m8all = rows_f[:, 576:640]
    gtmp1 = rows_f[:, 640:704]
    gtmp2 = rows_f[:, 704:768]
    gtmp3 = rows_f[:, 768:832]
    wv = rows_f[:, 832:864].bitcast(BF16)
    bm01 = sb("bm01", [128, 128], BF16)

    kexp_holder = {"kexp": kexp}
    aux_guard = [0, 0]

    def x_tile(xbuf, kt, tc=None):
        if tc is None:
            return xbuf[:, kt * TC:(kt + 1) * TC]
        return xbuf[:, kt * TC + tc * 512: kt * TC + (tc + 1) * 512]

    def wcol_tile(buf, kt):
        return buf[:, kt * 128:(kt + 1) * 128]

    msk = {"dma": 0, "red": 0, "mm": 0}

    def f_mask_dma(sy, hi):
        while msk["dma"] < hi:
            s = msk["dma"]
            kt, ncn = divmod(s, 8)
            if s >= 3:
                sy.wait_ge(s_mc.h, s - 2)
            s_mk4[s % 3].inc(sy.dma_start(
                mskraw[s % 3],
                maskT[kt * 128:(kt + 1) * 128, ncn * 512:(ncn + 1) * 512]), 16)
            msk["dma"] += 1

    def f_mask_red(v, hi):
        while msk["red"] < hi:
            s = msk["red"]
            kt, ncn = divmod(s, 8)
            b = s % 3
            v.wait_ge(s_mk4[b].h, 16 * (s // 3 + 1))
            src = mskraw[b].rearrange("p (nb w) -> p nb w", w=32)
            s_mc.inc(v.reduce_sum(
                knb_f[:, (kt % 2) * 128 + ncn * 16: (kt % 2) * 128 + (ncn + 1) * 16],
                src, axis=AX.X))
            if ncn == 7:
                if kt >= 3:
                    v.wait_ge(s_m1.h, kt - 2)
                v.drain()
                s_cvm.inc(v.tensor_copy(
                    knb_b[:, (kt % 3) * 128:(kt % 3 + 1) * 128],
                    knb_f[:, (kt % 2) * 128:(kt % 2 + 1) * 128]))
            msk["red"] += 1

    def f_mask_mm(t, hi, bv_gate):
        while msk["mm"] < hi:
            kt = msk["mm"]
            if kt == 0:
                t.wait_ge(s_bv.h, bv_gate)
            t.wait_ge(s_cvm.h, kt + 1)
            s_m1.inc(t.matmul(plow[1][:, 4 * kt:4 * kt + 4],
                              knb_b[:, (kt % 3) * 128:(kt % 3 + 1) * 128],
                              bpksb[:], start=True, stop=True, skip_group_check=True))
            msk["mm"] += 1

    def emit_aux_kt(kt, xbuf, do_gl, src_sem_h, src_val, extra_bv=None):
        guard = aux_guard[kt % 2]
        sq = sqb[kt % 2]

        def f_sc(sc):
            sc.wait_ge(src_sem_h, src_val)
            if guard:
                sc.wait_ge(s_ax.h, guard)
            s_sq.inc(sc.activation(sq[:], x_tile(xbuf, kt), AF.Square))
        block.scalar(f_sc)
        sqv = s_sq.n

        def f_t(t):
            t.wait_ge(s_sq.h, sqv)
            if extra_bv is not None:
                t.wait_ge(s_bv.h, extra_bv)
            last = (kt == KT - 1)
            for tc in range(2):
                li = t.matmul(pstat[64 * tc:64 * tc + 1, :], ones[:], x_tile(xbuf, kt, tc),
                              start=(kt == 0), stop=last,
                              tile_position=(0, 64 * tc), skip_group_check=True)
                li = t.matmul(pstat[64 * tc + 32:64 * tc + 33, :], ones[:],
                              sq[:, tc * 512:(tc + 1) * 512],
                              start=(kt == 0), stop=last, tile_position=(0, 64 * tc + 32), skip_group_check=True)
                if do_gl:
                    li = t.matmul(pgate[32 * tc:32 * tc + 8, :], gw_s[:, kt * 8:(kt + 1) * 8],
                                  x_tile(xbuf, kt, tc), start=(kt == 0),
                                  stop=False, tile_position=(0, 32 * tc), skip_group_check=True)
                    li = t.matmul(plow[tc][:], acat_s[:, kt * 128:(kt + 1) * 128],
                             x_tile(xbuf, kt, tc), start=(kt == 0), stop=False, skip_group_check=True)
            s_ax.inc(li)
        block.tensor(f_t)
        aux_guard[kt % 2] = s_ax.n

    def emit_boundary_chain(is_ffn):
        def f_v(v):
            for tc, p in ((0, 0), (1, 64)):
                v.tensor_scalar(rows_f[p:p + 1, 0:512], pstat[p:p + 1, :], INV_D, None, ALU.mult)
                v.tensor_scalar(rows_f[p + 32:p + 33, 0:512], pstat[p + 32:p + 33, :], INV_D, None, ALU.mult)
            v.drain()
            for tc, p in ((0, 0), (1, 64)):
                v.tensor_scalar(aux_bf[p:p + 1, 2560:3072], rows_f[p:p + 1, 0:512], -1.0, None, ALU.mult)
            s_bv.incd(v)
        block.vector(f_v)
        bv_var = s_bv.n

        def f_hop(sc):
            sc.wait_ge(s_bv.h, bv_var)
            for tc, p in ((0, 0), (1, 64)):
                s_bc.inc(sc.dma_start(rows_f[p:p + 1, 512:1024], rows_f[p + 32:p + 33, 0:512]), 16)
            sc.wait_ge(s_bc.h, s_bc.n)
        block.scalar(f_hop)
        bc_hop = s_bc.n

        def f_v1b(v):
            v.wait_ge(s_bc.h, bc_hop)
            for tc, p in ((0, 0), (1, 64)):
                v.tensor_mul(rows_f[p:p + 1, 1024:1536], rows_f[p:p + 1, 0:512], rows_f[p:p + 1, 0:512])
            v.drain()
            for tc, p in ((0, 0), (1, 64)):
                v.tensor_sub(rows_f[p:p + 1, 512:1024], rows_f[p:p + 1, 512:1024],
                             rows_f[p:p + 1, 1024:1536])
            s_bv.incd(v)
        block.vector(f_v1b)
        bv_var2 = s_bv.n

        def f_sc(sc):
            sc.wait_ge(s_bv.h, bv_var2)
            for tc, p in ((0, 0), (1, 64)):
                sc.activation(rows_f[p:p + 1, 1024:1536], rows_f[p:p + 1, 512:1024],
                              AF.Sqrt, bias=epsc[p:p + 1, :])
            s_bv.incd(sc)
        block.scalar(f_sc)
        bv_sig = s_bv.n

        def f_v2(v):
            v.wait_ge(s_bv.h, bv_sig)
            for tc, p in ((0, 0), (1, 64)):
                v.reciprocal(rows_f[p:p + 1, 1536:2048], rows_f[p:p + 1, 1024:1536])
            v.drain()
            sc1 = 0.5 if is_ffn else 1.0
            for tc, p in ((0, 0), (1, 64)):
                v.scalar_tensor_tensor(aux_bf[p:p + 1, 3072:3584],
                                       rows_f[p:p + 1, 1536:2048], sc1,
                                       aux_bf[p:p + 1, 2560:3072], ALU.mult, ALU.mult)
            s_bv.incd(v)
        block.vector(f_v2)
        bv_r = s_bv.n

        def f_d2a(sc):
            sc.wait_ge(s_bv.h, bv_var)
            if is_ffn:
                s_bc.inc(sc.dma_start(aux_bf[32:33, 3584:4096], aux_bf[0:1, 2560:3072]), 16)
                s_bc.inc(sc.dma_start(aux_bf[32:33, 4096:4608], aux_bf[64:65, 2560:3072]), 16)
            row = 64 if is_ffn else 96
            s_bc.inc(sc.dma_start(aux_bf[row:row + 1, 4096:4608], aux_bf[0:1, 2560:3072]), 16)
            s_bc.inc(sc.dma_start(aux_bf[row:row + 1, 4608:5120], aux_bf[64:65, 2560:3072]), 16)
            sc.wait_ge(s_bc.h, s_bc.n)
        block.scalar(f_d2a)
        bc_msl = s_bc.n

        def f_d2(sc):
            sc.wait_ge(s_bv.h, bv_r)
            s_bc.inc(sc.dma_start(dscr[0:1, 0:512], rows_f[0:1, 1536:2048]), 16)
            s_bc.inc(sc.dma_start(dscr[1:2, 0:512], rows_f[64:65, 1536:2048]), 16)
            s_bc.inc(sc.dma_start(dbr[0:1, :], aux_bf[0:1, 3072:3584]), 16)
            s_bc.inc(sc.dma_start(dbr[1:2, :], aux_bf[64:65, 3072:3584]), 16)
            sc.wait_ge(s_bc.h, s_bc.n)
            s_bc.inc(sc.dma_start(r_b[:, 0:512], dscr[0:1, 0:512].partition_broadcast(128)), 16)
            s_bc.inc(sc.dma_start(r_b[:, 512:1024], dscr[1:2, 0:512].partition_broadcast(128)), 16)
            s_bc.inc(sc.dma_start(rmb[:, 0:512], dbr[0:1, :].partition_broadcast(128)), 16)
            s_bc.inc(sc.dma_start(rmb[:, 512:1024], dbr[1:2, :].partition_broadcast(128)), 16)
            sc.wait_ge(s_bc.h, s_bc.n)
        block.scalar(f_d2)
        bc_done = s_bc.n

        if not is_ffn:
            return {"bc_done": bc_done, "bc_msl": bc_msl, "bv_end": bv_r, "bv_var": bv_var}

        def f_t(t):
            t.wait_ge(s_bc.h, bc_msl)
            for tc in range(2):
                msl = aux_bf[32:33, 3584 + tc * 512: 3584 + (tc + 1) * 512]
                t.matmul(pgate[32 * tc:32 * tc + 8, :], aux_bf[32:33, 128:136], msl,
                         start=False, stop=True, tile_position=(32, 32 * tc), skip_group_check=True)
                li = t.matmul(plow[tc][:], aux_bf[32:33, 0:128], msl,
                              start=False, stop=True, tile_position=(32, 0), skip_group_check=True)
            s_ax.inc(li)
        block.tensor(f_t)
        ax_mr = s_ax.n

        def f_v3(v):
            v.wait_ge(s_ax.h, ax_mr)
            v.wait_ge(s_bc.h, bc_done)
            v.tensor_mul(gstage[0:8, 0:512], pgate[0:8, :], r_b[0:8, 0:512])
            v.tensor_mul(gstage[32:40, 0:512], pgate[32:40, :], r_b[32:40, 512:1024])
            s_bv.incd(v)
        block.vector(f_v3)
        bv_gl = s_bv.n

        def f_t2(t):
            t.wait_ge(s_bv.h, bv_gl)
            for tt in range(8):
                half = tt // 4
                lhs = gstage[32 * half:32 * half + 8, (tt % 4) * 128:(tt % 4 + 1) * 128]
                rhs = idsb[32 * half:32 * half + 8, 32 * half:32 * half + 8]
                li = t.matmul(pstat[:, tt * 8:(tt + 1) * 8], lhs, rhs,
                              start=True, stop=True, tile_position=(32 * half, 0),
                              skip_group_check=True)
            s_ax.inc(li)
        block.tensor(f_t2)
        ax_tr = s_ax.n

        def f_v4(v):
            v.wait_ge(s_ax.h, ax_tr)
            v.tensor_copy(gv[:], pstat[:, 0:64])
            v.drain()
            for s in range(8):
                v.max(m8all[:, s * 8:(s + 1) * 8], gv[:, s * 8:(s + 1) * 8])
            v.drain()
            m3 = m8all[:].rearrange("p (s e) -> p s e", e=8)
            v.tensor_copy(gtmp1[:].rearrange("p (s e) -> p s e", e=8),
                          m3[:, :, 1:2].broadcast_to([128, 8, 8]))
            v.tensor_copy(gtmp2[:].rearrange("p (s e) -> p s e", e=8),
                          m3[:, :, 0:1].broadcast_to([128, 8, 8]))
            v.drain()
            v.tensor_tensor(gtmp3[:], gv[:], gtmp1[:], ALU.is_ge)
            v.tensor_sub(gtmp2[:], gv[:], gtmp2[:])
            s_bv.incd(v)
        block.vector(f_v4)
        bv_g1 = s_bv.n

        def f_sc2(sc):
            sc.wait_ge(s_bv.h, bv_g1)
            li = sc.activation(gtmp2[:], gtmp2[:], AF.Exp)
            s_bv.inc(li)
        block.scalar(f_sc2)
        bv_g2 = s_bv.n

        def f_v5(v):
            v.wait_ge(s_bv.h, bv_g2)
            v.tensor_mul(gtmp1[:], gtmp2[:], gtmp3[:])
            v.drain()
            v.reduce_sum(gtmp2[:, 0:8], gtmp1[:].rearrange("p (s e) -> p s e", e=8), axis=AX.X)
            v.drain()
            v.reciprocal(gtmp2[:, 8:16], gtmp2[:, 0:8])
            v.drain()
            rdex = gtmp2[:, 8:16].rearrange("p (s o) -> p s o", o=1).broadcast_to([128, 8, 8])
            v.tensor_copy(gtmp3[:].rearrange("p (s e) -> p s e", e=8), rdex)
            v.drain()
            v.scalar_tensor_tensor(wv[:], gtmp1[:], 2.0, gtmp3[:], ALU.mult, ALU.mult)
            s_bv.incd(v)
        block.vector(f_v5)
        bv_g3 = s_bv.n

        def f_t3(t):
            t.wait_ge(s_bv.h, bv_g3)
            for tt in range(8):
                row = 0 if tt < 4 else 32
                li = t.matmul(pgate[row:row + 8, (tt % 4) * 128:(tt % 4 + 1) * 128],
                              wv[:, tt * 8:(tt + 1) * 8], idsb[:, 0:128],
                              start=True, stop=True, tile_position=(0, row),
                              skip_group_check=True)
            s_ax.inc(li)
        block.tensor(f_t3)
        ax_wt = s_ax.n

        def f_v6(v):
            v.wait_ge(s_ax.h, ax_wt)
            v.tensor_copy(aux_bf[0:8, 0:512], pgate[0:8, :])
            v.drain()
            v.tensor_copy(aux_bf[0:8, 512:1024], pgate[32:40, :])
            s_bv.incd(v)
        block.vector(f_v6)
        bv_wf = s_bv.n

        def f_d3(sc):
            sc.wait_ge(s_bv.h, bv_wf)
            s_bc.inc(sc.dma_start(dbw, aux_bf[0:8, 0:1024]), 16)
            sc.wait_ge(s_bc.h, s_bc.n)
            for e in range(8):
                s_bc.inc(sc.dma_start(w2exp[e * 16:(e + 1) * 16, :],
                                      dbw[e:e + 1, :].partition_broadcast(16)), 16)
            sc.wait_ge(s_bc.h, s_bc.n)
        block.scalar(f_d3)
        bc_w2 = s_bc.n

        def f_v7(v):
            v.wait_ge(s_bc.h, bc_w2)
            for tc in range(2):
                v.tensor_mul(zs[:, tc * 512:(tc + 1) * 512], plow[tc][:],
                             w2exp[:, tc * 512:(tc + 1) * 512])
            s_bv.incd(v)
        block.vector(f_v7)
        return {"bc_done": bc_done, "bc_msl": bc_msl, "bv_end": s_bv.n, "bv_var": bv_var}

    def emit_sublayer(name, mode, chunk, xsrc, xdst, aux_next, jit_bsum=False,
                      mask_s1=False, x1_load=False, skip_boundary=False):
        is_ffn = (mode == "ffn")
        col_ready = {}
        prev_mm = s_mm.n
        ep_base = s_ep.n
        mm_base = s_mm.n
        wm_base = s_wm.n
        ws_base = s_ws.n
        scw_base = [s_scw2[0].n, s_scw2[1].n]
        out_base = [s_out2[0].n, s_out2[1].n]
        scw_tot = (s_scw2[0].n, s_scw2[1].n)

        cs_ax = {}

        def issue_col_dma(eng, nt):
            if nt < 2:
                eng.wait_ge(s_mm.h, prev_mm - 2 + 2 * nt if prev_mm >= 2 else prev_mm)
                eng.wait_ge(s_scw2[0].h, scw_tot[0])
                eng.wait_ge(s_scw2[1].h, scw_tot[1])
            if nt >= 2:
                eng.wait_ge(s_mm.h, mm_base + 2 * (nt - 1))
                if mode == "attn_jit":
                    eng.wait_ge(s_scw2[nt % 2].h, scw_base[nt % 2] + 16 * (nt // 2))
            dst = wcol[nt % 2]
            if mode == "ffn":
                src = baseT[nt]
            elif mode == "attn_jit":
                src = wT[nt]
            else:
                src = scrA[nt]
            wsem = s_wcs2[nt % 2] if mode == "attn_scr" else s_wc2[nt % 2]
            wsem.inc(eng.dma_start(dst[0:64, :], src[0:64]), 16)
            wsem.inc(eng.dma_start(dst[64:128, :], src[64:128]), 16)
            col_ready[nt] = (wsem.h, wsem.n)

        col_q = block.sync if mode == "attn_scr" else block.gpsimd

        def f_cols01(eng):
            issue_col_dma(eng, 0)
            issue_col_dma(eng, 1)
        col_q(f_cols01)
        barrier()

        jit = (mode == "attn_jit" or jit_bsum)
        rot = 3 if (mode == "attn_jit" or not ROT4) else 4

        def emit_k(g):
            nt, tc = g // 2, g % 2
            pm = pmain[g % rot]
            plain_attn = (not is_ffn and not jit)

            def f_k(t):
                if g < rot:
                    t.wait_ge(s_ep.h, ep_base)
                else:
                    t.wait_ge(s_ep.h, ep_base + g - (rot - 1))
                if mode == "attn_jit":
                    t.wait_ge(s_wm.h, wm_base + nt + 1)
                else:
                    t.wait_ge(*col_ready[nt])
                li = None
                for kt in range(KT):
                    li = t.matmul(pm[:], wcol_tile(wcol[nt % 2], kt), x_tile(xsrc, kt, tc),
                                  start=(kt == 0), stop=(kt == KT - 1 and plain_attn),
                                  skip_group_check=True)
                if plain_attn:
                    s_mm.inc(li)
            block.tensor(f_k)

        def emit_fin2(g, bnd=None):
            if not is_ffn and not jit:
                return
            nt, tc = g // 2, g % 2
            pm = pmain[g % rot]

            def f_fin(t):
                if bnd is not None:
                    if jit:
                        t.wait_ge(s_bc.h, bnd["bc_msl"])
                    if is_ffn:
                        t.wait_ge(s_bv.h, bnd["bv_end"])
                li = None
                if jit:
                    t.wait_ge(s_ws.h, ws_base + nt + 1)
                    mrow = 64 if is_ffn else 96
                    li = t.matmul(pm[:], aux_bf[mrow:mrow + 1, nt * 128:(nt + 1) * 128],
                                  aux_bf[mrow:mrow + 1, 4096 + tc * 512:4096 + (tc + 1) * 512],
                                  start=False, stop=(not is_ffn), tile_position=(mrow, 0),
                                  skip_group_check=True)
                if is_ffn:
                    li = t.matmul(pm[:], bcat_s[:, nt * 128:(nt + 1) * 128],
                                  zs[:, tc * 512:(tc + 1) * 512], start=False, stop=True,
                                  skip_group_check=True)
                s_mm.inc(li)
            block.tensor(f_fin)

        def emit_ep(g):
            nt, tc = g // 2, g % 2
            pm = pmain[g % rot]
            mmv = s_mm.n

            xpm = (wsum_pm if mode.startswith("attn") else bsum_pm)

            def f_ep(v):
                v.wait_ge(s_mm.h, mmv)
                if g == 0:
                    v.wait_ge(s_bc.h, bnd["bc_done"])
                sc1 = 0.5 if is_ffn else 1.0
                v.scalar_tensor_tensor(pm[:], pm[:], sc1, r_b[:, tc * 512:(tc + 1) * 512],
                                       ALU.mult, ALU.mult)
                v.drain()
                if not jit:
                    v.scalar_tensor_tensor(pm[:], rmb[:, tc * 512:(tc + 1) * 512],
                                           xpm[:, nt:nt + 1], pm[:], ALU.mult, ALU.add)
                    v.drain()
                bias_ap = (biasBh if is_ffn else biasA)[:, nt:nt + 1]
                if xdst is not None:
                    li = v.scalar_tensor_tensor(x_tile(xdst, nt, tc), pm[:], bias_ap,
                                                x_tile(xsrc, nt, tc), ALU.add, ALU.add)
                else:
                    v.wait_ge(s_out2[g % 2].h, out_base[g % 2] + 16 * (g // 2))
                    li = v.scalar_tensor_tensor(ostg[g % 2][:], pm[:], bias_ap,
                                                x_tile(xsrc, nt, tc), ALU.add, ALU.add)
                s_ep.inc(li)
            block.vector(f_ep)
            epv = s_ep.n

            if xdst is None:
                def f_out(sy):
                    sy.wait_ge(s_ep.h, epv)
                    dst = outT[nt * 128:(nt + 1) * 128,
                               chunk * TC + tc * 512: chunk * TC + (tc + 1) * 512]
                    s_out2[g % 2].inc(sy.dma_start(dst, ostg[g % 2][:]), 16)
                block.sync(f_out)

        if mode != "attn_jit":
            for g0 in range(2 if is_ffn else 0):
                emit_k(g0)

        bnd = emit_boundary_chain(is_ffn)

        aux_done = [0]

        def emit_aux_upto(upto):
            while aux_done[0] < upto:
                kt = aux_done[0]
                extra = bnd["bv_end"] if kt == 0 else None
                if aux_next[0] == "ep":
                    emit_aux_kt(kt, xdst, aux_next[1], s_ep.h, ep_base + 2 * kt + 2, extra)
                else:
                    sh, sv = xq_ready[kt // 4]
                    emit_aux_kt(kt, aux_next[2], aux_next[1], sh, sv, extra)
                aux_done[0] += 1

        for nt in range(NT):
            if x1_load and nt in (0, 2, 4, 6, 8, 10, 12, 14):
                q = nt // 2

                def f_x1(g, q=q):
                    src = xT_in[q * 4 * 128:(q + 1) * 4 * 128, TC:2 * TC]
                    src = src.rearrange("(kt p) t -> p kt t", p=128)
                    dst = xB[:, q * 4 * TC:(q + 1) * 4 * TC].rearrange("p (kt t) -> p kt t", t=TC)
                    sem = s_xq[q]
                    sem.inc(g.dma_start(dst, src), 16)
                    xq_ready[q] = (sem.h, sem.n)
                block.gpsimd(f_x1)

            if mode == "attn_jit":
                def f_vm(v, nt=nt):
                    v.wait_ge(*col_ready[nt])
                    if nt == 0:
                        v.wait_ge(s_kx.h, kexp_holder["ready"])
                    kexp = kexp_holder["kexp"]
                    src1 = kexp[:].rearrange("p (kt nb) -> p kt nb", nb=128)[:, :, 4 * nt:4 * nt + 4]
                    src1 = src1.rearrange("p kt (nb o) -> p kt nb o", o=1).broadcast_to([128, KT, 4, 32])
                    dst = wcol[nt % 2][:].rearrange("p (kt nb r) -> p kt nb r", nb=4, r=32)
                    li = v.tensor_tensor(dst, dst, src1, ALU.mult)
                    s_wm.inc(li)
                block.vector(f_vm)

                def f_scw(sy, nt=nt):
                    sy.wait_ge(s_wm.h, wm_base + nt + 1)
                    s_scw2[nt % 2].inc(sy.dma_start(scrA[nt], wcol[nt % 2][:]), 16)
                block.sync(f_scw)

            if mode == "attn_jit" or jit_bsum:
                row = 96 if mode == "attn_jit" else 64
                pcs = pmain[3] if mode == "attn_jit" else plow[0]
                xpm = wsum_pm if mode == "attn_jit" else bsum_pm

                pcs2 = pcs if nt % 2 == 0 else pgate

                def f_cs(t, nt=nt, row=row, pcs2=pcs2):
                    if nt == 0:
                        t.wait_ge(s_bv.h, bnd["bv_end"])
                    if mode == "attn_jit":
                        t.wait_ge(s_wm.h, wm_base + nt + 1)
                    else:
                        t.wait_ge(*col_ready[nt])
                    if nt > 1:
                        t.wait_ge(s_ws.h, ws_base + nt - 1)
                    for j in range(8):
                        li = t.matmul(pcs2[row:row + 1, :], ones[:],
                                      wcol[nt % 2][:, j * 512:(j + 1) * 512],
                                      start=(j == 0), stop=(j == 7),
                                      tile_position=(0, row), skip_group_check=True)
                    s_ax.inc(li)
                block.tensor(f_cs)
                ax_cs = s_ax.n
                cs_ax[nt] = ax_cs

                def f_csv(v, nt=nt, row=row, pcs2=pcs2):
                    v.wait_ge(s_ax.h, ax_cs)
                    src = pcs2[row:row + 1, :].rearrange("p (a n) -> p n a", a=4)
                    v.reduce_sum(rows_f[row:row + 1, 0:128], src, axis=AX.X)
                    v.drain()
                    li = v.tensor_copy(aux_bf[row:row + 1, nt * 128:(nt + 1) * 128],
                                       rows_f[row:row + 1, 0:128])
                    s_ws.inc(li)
                block.vector(f_csv)

            if mask_s1:
                red_hi = min(256, 16 + ((nt + 1) * 240 + 23) // 24)
                dma_hi = min(256, red_hi + 8)
                mm_hi = min(32, max(0, red_hi // 8 - 2))
                block.sync(lambda sy, h=dma_hi: f_mask_dma(sy, h))
                block.vector(lambda v, h=red_hi: f_mask_red(v, h))
                block.tensor(lambda t, h=mm_hi: f_mask_mm(t, h, bnd["bv_end"]))
                if nt == 25:
                    block.tensor(lambda t: f_mask_mm(t, 32, bnd["bv_end"]))

                    def f_bm_thr(v):
                        v.wait_ge(s_m1.h, 32)
                        s_bv.inc(v.tensor_scalar(bm01T[:], plow[1][:, 0:128], 0.5,
                                                 None, ALU.is_gt))
                    block.vector(f_bm_thr)
                    bv_thr = s_bv.n

                    def f_bm_tr(t):
                        t.wait_ge(s_bv.h, bv_thr)
                        s_m1.inc(t.matmul(plow[1][:, 128:256], bm01T[:], idsb[:, 0:128],
                                          start=True, stop=True, skip_group_check=True))
                    block.tensor(f_bm_tr)

                    def f_bm_ev(v):
                        v.wait_ge(s_m1.h, 33)
                        s_bv.inc(v.tensor_copy(bm01[:], plow[1][:, 128:256]))
                    block.vector(f_bm_ev)
                    bv_bm = s_bv.n

                    def f_kexp(sy):
                        sy.wait_ge(s_bv.h, bv_bm)
                        s_kx.inc(sy.dma_start(dbm, bm01[:]), 16)
                        sy.wait_ge(s_kx.h, s_kx.n)
                        kexp = kexp_holder["kexp"]
                        dbmv = dbm.rearrange("(kt j) n -> j kt n", j=4)
                        for j in range(4):
                            src = dbmv[j:j + 1].partition_broadcast(32)
                            s_kx.inc(sy.dma_start(kexp[32 * j:32 * j + 32, :], src), 16)
                        sy.wait_ge(s_kx.h, s_kx.n)
                    block.sync(f_kexp)
                    kexp_holder["ready"] = s_kx.n

            if nt + 2 < NT:
                col_q(lambda eng, n=nt + 2: issue_col_dma(eng, n))

            if aux_next is not None and nt >= 3:
                emit_aux_upto(nt - 2)

            if mode == "attn_jit" or not ROT4:
                if nt == 0:
                    if not is_ffn and mode == "attn_jit":
                        emit_k(0)
                        emit_fin2(0, bnd)
                        emit_ep(0)
                        emit_k(1)
                        emit_fin2(1, bnd)
                        emit_ep(1)
                    elif is_ffn:
                        emit_fin2(0, bnd)
                        emit_ep(0)
                        emit_fin2(1, bnd)
                        emit_ep(1)
                    else:
                        emit_k(0)
                        emit_fin2(0, bnd)
                        emit_ep(0)
                        emit_k(1)
                        emit_fin2(1, bnd)
                        emit_ep(1)
                else:
                    for tc in range(2):
                        g = 2 * nt + tc
                        emit_k(g)
                        emit_fin2(g)
                        emit_ep(g)
            elif nt == 0:
                if is_ffn:
                    emit_fin2(0, bnd)
                    emit_ep(0)
                    emit_fin2(1, bnd)
                    emit_ep(1)
                else:
                    emit_k(0)
                    emit_fin2(0, bnd)
                    emit_ep(0)
                    emit_k(1)
                    emit_fin2(1, bnd)
                    emit_ep(1)
            else:
                for tc in range(2):
                    g = 2 * nt + tc
                    emit_k(g)
                    emit_fin2(g)
                    emit_ep(g)

        if jit:
            row = 96 if mode == "attn_jit" else 64
            xpm2 = wsum_pm if mode == "attn_jit" else bsum_pm
            ptr = pmain[3] if mode == "attn_jit" else pgate

            def f_tr(t):
                t.wait_ge(s_ws.h, ws_base + NT)
                for nt2 in range(NT):
                    li = t.matmul(ptr[:, nt2:nt2 + 1],
                                  aux_bf[row:row + 1, nt2 * 128:(nt2 + 1) * 128],
                                  ones[row:row + 1, 0:1], start=True, stop=True,
                                  tile_position=(row, 0), skip_group_check=True)
                s_ax.inc(li)
            block.tensor(f_tr)
            axtr = s_ax.n

            def f_tre(v):
                v.wait_ge(s_ax.h, axtr)
                li = v.tensor_copy(xpm2[:, 0:NT], ptr[:, 0:NT])
                s_bv.inc(li)
            block.vector(f_tre)

        if aux_next is not None:
            emit_aux_upto(KT)

    def f_p0_sync(sy):
        s_ld.inc(sy.dma_start(idsb[:], ident.ap()), 16)
        s_ld.inc(sy.dma_start(ones[:], ones_c.ap()), 16)
        s_ld.inc(sy.dma_start(bpksb[:], bpk.ap()), 16)
        s_ld.inc(sy.dma_start(biasA[:], biasA_d), 16)
        s_ld.inc(sy.dma_start(biasB_t[:], biasB_d), 16)
        f_mask_dma(sy, 19)
    block.sync(f_p0_sync)
    block.vector(lambda v: f_mask_red(v, 16))

    def f_p0_g(g):
        s_ldg.inc(g.dma_start(gw_s[:].rearrange("p (kt e) -> p kt e", e=8),
                              gwT_d.rearrange("(kt p) e -> p kt e", p=128)), 16)
        s_ldg.inc(g.dma_start(acat_s[:].rearrange("p (kt e) -> p kt e", e=128),
                              acatT_d.rearrange("(kt p) e -> p kt e", p=128)), 16)
        s_ldg.inc(g.dma_start(bcat_s[:], bcat2_d), 16)
        for q in range(16):
            src = xT_in[q * 2 * 128:(q + 1) * 2 * 128, 0:TC]
            src = src.rearrange("(kt p) t -> p kt t", p=128)
            dst = xA[:, q * 2 * TC:(q + 1) * 2 * TC].rearrange("p (kt t) -> p kt t", t=TC)
            sem = s_xq[q]
            sem.inc(g.dma_start(dst, src), 16)
            xq_ready[q] = (sem.h, sem.n)
    block.gpsimd(f_p0_g)
    ld_all = s_ld.n
    ldg_all = s_ldg.n

    def f_p0_v(v):
        v.memset(epsc[:], EPS)
        v.wait_ge(s_ld.h, ld_all)
        v.wait_ge(s_ldg.h, ldg_all)
        li = v.tensor_scalar(biasBh[:], biasB_t[:], 0.5, None, ALU.mult)
        s_bv.inc(li)
    block.vector(f_p0_v)

    def f_p0_t(t):
        t.wait_ge(s_ld.h, ld_all)
        t.wait_ge(s_ldg.h, ldg_all)
        for kt in range(KT):
            t.matmul(pmain[0][32:33, 0:128], ones[:], acat_s[:, kt * 128:(kt + 1) * 128],
                     start=(kt == 0), stop=(kt == KT - 1), tile_position=(0, 32), skip_group_check=True)
            li = t.matmul(pmain[1][32:33, 0:8], ones[:], gw_s[:, kt * 8:(kt + 1) * 8],
                     start=(kt == 0), stop=(kt == KT - 1), tile_position=(0, 32), skip_group_check=True)
        s_ax.inc(li)
    block.tensor(f_p0_t)
    ax_p0 = s_ax.n

    def f_p0_v2(v):
        v.wait_ge(s_ax.h, ax_p0)
        v.tensor_copy(aux_bf[32:33, 0:128], pmain[0][32:33, 0:128])
        v.tensor_copy(aux_bf[32:33, 128:136], pmain[1][32:33, 0:8])
        s_bv.incd(v)
    block.vector(f_p0_v2)

    for kt in range(KT):
        emit_aux_kt(kt, xA, True, *xq_ready[kt // 2])

    emit_sublayer("s1c0", "ffn", 0, xA, xB, ("ep", False), jit_bsum=True, mask_s1=True)
    emit_sublayer("s2c0", "attn_jit", 0, xB, xA, ("ep", True))
    emit_sublayer("s3c0", "ffn", 0, xA, None, ("dma", True, xB), x1_load=True)

    emit_sublayer("s1c1", "ffn", 1, xB, xA, ("ep", False))
    emit_sublayer("s2c1", "attn_scr", 1, xA, xB, ("ep", True))
    emit_sublayer("s3c1", "ffn", 1, xB, None, None)

    o0, o1 = s_out2[0].n, s_out2[1].n

    def f_tail(sy):
        sy.wait_ge(s_out2[0].h, o0)
        sy.wait_ge(s_out2[1].h, o1)
    block.sync(f_tail)

    ctx.close()
    return nc


def kernel(L0_attn_data, L0_attn_mask, L0_attn_weight, L0_attn_bias,
           L0_ffn_gate_w, L0_ffn_base_w, L0_ffn_base_b, L0_ffn_lora_A,
           L0_ffn_lora_B):
    data = np.asarray(L0_attn_data, dtype=np.float32).reshape(16384, D)
    maskT_h = np.ascontiguousarray(np.asarray(L0_attn_mask, dtype=np.int32).T)
    def colblocks(w):
        wt = np.asarray(w, dtype=np.float32).T.reshape(KT, 128, NT, 128)
        return np.ascontiguousarray(wt.transpose(2, 1, 0, 3)).reshape(NT, 128, KT * 128)

    wT_h = colblocks(L0_attn_weight)
    baseT_h = colblocks(L0_ffn_base_w)
    gwT_h = np.ascontiguousarray(np.asarray(L0_ffn_gate_w, dtype=np.float32).T)
    acatT_h = np.ascontiguousarray(
        np.asarray(L0_ffn_lora_A, dtype=np.float32).reshape(128, D).T)
    bcat2_h = np.ascontiguousarray(
        np.asarray(L0_ffn_lora_B, dtype=np.float32).transpose(0, 2, 1).reshape(128, D))
    biasA_h = np.ascontiguousarray(np.asarray(L0_attn_bias, dtype=np.float32).reshape(32, 128).T)
    biasB_h = np.ascontiguousarray(np.asarray(L0_ffn_base_b, dtype=np.float32).reshape(32, 128).T)

    nc = build_program()
    in_maps = []
    for c in range(NCORES):
        xT = np.ascontiguousarray(data[c * TPC:(c + 1) * TPC, :].T)
        in_maps.append({
            "xT": xT, "maskT": maskT_h, "wT": wT_h, "baseT": baseT_h, "gwT": gwT_h,
            "acatT": acatT_h, "bcat2": bcat2_h, "biasA": biasA_h, "biasB": biasB_h,
        })
    res = run_bass_kernel_spmd(nc, in_maps, core_ids=list(range(NCORES)))
    parts = [res.results[c]["outT"].T for c in range(NCORES)]
    out = np.concatenate(parts, axis=0).reshape(2, 8192, D).astype(np.float32)
    return out


# version 19
# speedup vs baseline: 1.0506x; 1.0506x over previous
import numpy as np
import ml_dtypes
from contextlib import ExitStack

import concourse.bass as bass
import concourse.mybir as mybir
from concourse.bass_utils import run_bass_kernel_spmd

F32 = mybir.dt.float32
BF16 = mybir.dt.bfloat16
I32 = mybir.dt.int32
AF = mybir.ActivationFunctionType
ALU = mybir.AluOpType
AX = mybir.AxisListType
bfdt = ml_dtypes.bfloat16

ROT4 = True
D = 4096
KT = 32
NT = 32
TPC = 2048
TC = 1024
NCORES = 8
INV_D = 1.0 / D
EPS = 1e-5


class Sem:
    def __init__(self, h):
        self.h = h
        self.n = 0

    def inc(self, inst, by=1):
        inst.then_inc(self.h, by)
        self.n += by
        return self.n

    def incd(self, eng, by=1):
        eng.drain()
        eng.sem_inc(self.h, by)
        self.n += by
        return self.n


def build_program():
    nc = bass.Bass("TRN2", target_bir_lowering=False)

    xT_in = nc.dram_tensor("xT", [D, TPC], F32, kind="ExternalInput").ap()
    maskT = nc.dram_tensor("maskT", [D, D], I32, kind="ExternalInput").ap()
    wT = nc.dram_tensor("wT", [NT, 128, KT * 128], F32, kind="ExternalInput").ap()
    baseT = nc.dram_tensor("baseT", [NT, 128, KT * 128], F32, kind="ExternalInput").ap()
    gwT_d = nc.dram_tensor("gwT", [D, 8], F32, kind="ExternalInput").ap()
    acatT_d = nc.dram_tensor("acatT", [D, 128], F32, kind="ExternalInput").ap()
    bcat2_d = nc.dram_tensor("bcat2", [128, D], F32, kind="ExternalInput").ap()
    biasA_d = nc.dram_tensor("biasA", [128, 32], F32, kind="ExternalInput").ap()
    biasB_d = nc.dram_tensor("biasB", [128, 32], F32, kind="ExternalInput").ap()
    outT = nc.dram_tensor("outT", [D, TPC], F32, kind="ExternalOutput").ap()
    scrA = nc.dram_tensor("scrA", [NT, 128, KT * 128], BF16, kind="Internal").ap()
    dscr = nc.dram_tensor("dscr", [16, 1024], F32, kind="Internal").ap()
    dbm = nc.dram_tensor("dbm", [128, 128], BF16, kind="Internal").ap()
    dbw = nc.dram_tensor("dbw", [8, TC], BF16, kind="Internal").ap()
    dbr = nc.dram_tensor("dbr", [2, 512], BF16, kind="Internal").ap()

    ident = nc.inline_tensor(np.eye(128, dtype=bfdt), "ident")
    ones_c = nc.inline_tensor(np.ones((128, 1), dtype=bfdt), "ones_c")
    bpk_np = np.zeros((128, 4), dtype=bfdt)
    for p in range(128):
        bpk_np[p, p // 32] = 1.0
    bpk = nc.inline_tensor(bpk_np, "bpk")

    ctx = ExitStack()
    sb = lambda nm, sh, dt: ctx.enter_context(nc.sbuf_tensor(nm, sh, dt))

    xA = sb("xA", [128, KT * TC], BF16)
    xB = sb("xB", [128, KT * TC], BF16)
    wcol = [sb(f"wcol{i}", [128, KT * 128], BF16) for i in range(2)]
    acat_s = sb("acat_s", [128, KT * 128], BF16)
    bcat_s = sb("bcat_s", [128, D], BF16)
    gw_s = sb("gw_s", [128, KT * 8], BF16)
    idsb = sb("idsb", [128, 128], BF16)
    ones = sb("ones", [128, 1], BF16)
    bpksb = sb("bpksb", [128, 4], BF16)
    biasA = sb("biasA_s", [128, 32], F32)
    biasBh = sb("biasBh", [128, 32], F32)
    biasB_t = sb("biasB_t", [128, 32], F32)
    aux_bf = sb("aux_bf", [128, 5120], BF16)
    rows_f = sb("rows_f", [128, 2048], F32)
    r_b = sb("r_b", [128, TC], F32)
    w2exp = sb("w2exp", [128, TC], BF16)
    rmb = sb("rmb", [128, TC], BF16)
    bsum_pm = sb("bsum_pm", [128, 32], F32)
    wsum_pm = sb("wsum_pm", [128, 32], F32)
    zs = sb("zs", [128, TC], BF16)
    sqb = [sb(f"sqb{i}", [128, TC], BF16) for i in range(2)]
    gstage = sb("gstage", [128, 512], BF16)
    epsc = sb("epsc", [128, 1], F32)
    ostg = [sb(f"ostg{i}", [128, 512], F32) for i in range(2)]
    kexp = sb("kexp", [128, KT * 128], BF16)

    pmain = [ctx.enter_context(nc.psum_tensor(f"pm{i}", [128, 512], F32)) for i in range(4)]
    pstat = ctx.enter_context(nc.psum_tensor("pstat", [128, 512], F32))
    pgate = ctx.enter_context(nc.psum_tensor("pgate", [128, 512], F32))
    plow = [ctx.enter_context(nc.psum_tensor(f"pl{i}", [128, 512], F32)) for i in range(2)]

    names = ["s_wcs", "s_bc", "s_scw", "s_out", "s_ld", "s_ldg",
             "s_mm", "s_ep", "s_sq", "s_ax", "s_wm", "s_ws", "s_m1", "s_m1e", "s_bv"]
    S = {nm: Sem(ctx.enter_context(nc.semaphore(nm))) for nm in names}
    s_wcs, s_bc, s_scw, s_out, s_ld, s_ldg = (S[n] for n in names[:6])
    s_mm, s_ep, s_sq, s_ax, s_wm, s_ws, s_m1, s_m1e, s_bv = (S[n] for n in names[6:])
    s_xq = [Sem(ctx.enter_context(nc.semaphore(f"s_xq{i}"))) for i in range(16)]
    s_wc2 = [Sem(ctx.enter_context(nc.semaphore(f"s_wc{i}"))) for i in range(2)]
    s_mk4 = [Sem(ctx.enter_context(nc.semaphore(f"s_mk{i}"))) for i in range(4)]
    s_cvm = Sem(ctx.enter_context(nc.semaphore("s_cvm")))
    s_wcs2 = [Sem(ctx.enter_context(nc.semaphore(f"s_wcs{i}"))) for i in range(2)]
    s_scw2 = [Sem(ctx.enter_context(nc.semaphore(f"s_scw{i}"))) for i in range(2)]
    s_out2 = [Sem(ctx.enter_context(nc.semaphore(f"s_out{i}"))) for i in range(2)]
    s_kx = Sem(ctx.enter_context(nc.semaphore("s_kx")))
    s_cs2 = [Sem(ctx.enter_context(nc.semaphore(f"s_cs{i}"))) for i in range(2)]
    s_mc = Sem(ctx.enter_context(nc.semaphore("s_mc")))
    xq_ready = [None] * 16
    s_bar = Sem(ctx.enter_context(nc.semaphore("s_bar")))
    bar_k = [0]

    def barrier():
        bar_k[0] += 1
        tgt = 5 * bar_k[0]

        def mk(eng):
            eng.drain()
            eng.sem_inc(s_bar.h, 1)
            eng.wait_ge(s_bar.h, tgt)
        block.tensor(mk)
        block.vector(mk)
        block.scalar(mk)
        block.gpsimd(mk)
        block.sync(mk)
        s_bar.n = tgt

    block = ctx.enter_context(nc.Block())

    mskraw = [kexp[:, i * 1024:(i + 1) * 1024].bitcast(I32) for i in range(3)]
    knb_f = kexp[:, 3072:3584].bitcast(F32)
    knb_b = kexp[:, 3584:3968]
    bm01T = kexp[:, 3968:4096]
    gv = rows_f[:, 512:576]
    wvrep = aux_bf[:, 0:1024]
    m8all = rows_f[:, 576:640]
    gtmp1 = rows_f[:, 640:704]
    gtmp2 = rows_f[:, 704:768]
    gtmp3 = rows_f[:, 768:832]
    wv = rows_f[:, 832:864].bitcast(BF16)
    bm01 = sb("bm01", [128, 128], BF16)

    kexp_holder = {"kexp": kexp}
    aux_guard = [0, 0]

    def x_tile(xbuf, kt, tc=None):
        if tc is None:
            return xbuf[:, kt * TC:(kt + 1) * TC]
        return xbuf[:, kt * TC + tc * 512: kt * TC + (tc + 1) * 512]

    def wcol_tile(buf, kt):
        return buf[:, kt * 128:(kt + 1) * 128]

    msk = {"dma": 0, "red": 0, "mm": 0}

    def f_mask_dma(sy, hi):
        while msk["dma"] < hi:
            s = msk["dma"]
            kt, ncn = divmod(s, 8)
            if s >= 3:
                sy.wait_ge(s_mc.h, s - 2)
            s_mk4[s % 3].inc(sy.dma_start(
                mskraw[s % 3],
                maskT[kt * 128:(kt + 1) * 128, ncn * 512:(ncn + 1) * 512]), 16)
            msk["dma"] += 1

    def f_mask_red(v, hi):
        while msk["red"] < hi:
            s = msk["red"]
            kt, ncn = divmod(s, 8)
            b = s % 3
            v.wait_ge(s_mk4[b].h, 16 * (s // 3 + 1))
            src = mskraw[b].rearrange("p (nb w) -> p nb w", w=32)
            s_mc.inc(v.reduce_sum(
                knb_f[:, (kt % 2) * 128 + ncn * 16: (kt % 2) * 128 + (ncn + 1) * 16],
                src, axis=AX.X))
            if ncn == 7:
                if kt >= 3:
                    v.wait_ge(s_m1.h, kt - 2)
                v.drain()
                s_cvm.inc(v.tensor_copy(
                    knb_b[:, (kt % 3) * 128:(kt % 3 + 1) * 128],
                    knb_f[:, (kt % 2) * 128:(kt % 2 + 1) * 128]))
            msk["red"] += 1

    def f_mask_mm(t, hi, bv_gate):
        while msk["mm"] < hi:
            kt = msk["mm"]
            if kt == 0:
                t.wait_ge(s_bv.h, bv_gate)
            t.wait_ge(s_cvm.h, kt + 1)
            s_m1.inc(t.matmul(plow[1][:, 4 * kt:4 * kt + 4],
                              knb_b[:, (kt % 3) * 128:(kt % 3 + 1) * 128],
                              bpksb[:], start=True, stop=True, skip_group_check=True))
            msk["mm"] += 1

    def emit_aux_kt(kt, xbuf, do_gl, src_sem_h, src_val, extra_bv=None):
        guard = aux_guard[kt % 2]
        sq = sqb[kt % 2]

        def f_sc(sc):
            sc.wait_ge(src_sem_h, src_val)
            if guard:
                sc.wait_ge(s_ax.h, guard)
            s_sq.inc(sc.activation(sq[:], x_tile(xbuf, kt), AF.Square))
        block.scalar(f_sc)
        sqv = s_sq.n

        def f_t(t):
            t.wait_ge(s_sq.h, sqv)
            if extra_bv is not None:
                t.wait_ge(s_bv.h, extra_bv)
            last = (kt == KT - 1)
            for tc in range(2):
                li = t.matmul(pstat[64 * tc:64 * tc + 1, :], ones[:], x_tile(xbuf, kt, tc),
                              start=(kt == 0), stop=last,
                              tile_position=(0, 64 * tc), skip_group_check=True)
                li = t.matmul(pstat[64 * tc + 32:64 * tc + 33, :], ones[:],
                              sq[:, tc * 512:(tc + 1) * 512],
                              start=(kt == 0), stop=last, tile_position=(0, 64 * tc + 32), skip_group_check=True)
                if do_gl:
                    li = t.matmul(pgate[32 * tc:32 * tc + 8, :], gw_s[:, kt * 8:(kt + 1) * 8],
                                  x_tile(xbuf, kt, tc), start=(kt == 0),
                                  stop=False, tile_position=(0, 32 * tc), skip_group_check=True)
                    li = t.matmul(plow[tc][:], acat_s[:, kt * 128:(kt + 1) * 128],
                             x_tile(xbuf, kt, tc), start=(kt == 0), stop=False, skip_group_check=True)
            s_ax.inc(li)
        block.tensor(f_t)
        aux_guard[kt % 2] = s_ax.n

    def emit_boundary_chain(is_ffn):
        def f_v(v):
            for tc, p in ((0, 0), (1, 64)):
                v.tensor_scalar(rows_f[p:p + 1, 0:512], pstat[p:p + 1, :], INV_D, None, ALU.mult)
                v.tensor_scalar(rows_f[p + 32:p + 33, 0:512], pstat[p + 32:p + 33, :], INV_D, None, ALU.mult)
            v.drain()
            for tc, p in ((0, 0), (1, 64)):
                v.tensor_scalar(aux_bf[p:p + 1, 2560:3072], rows_f[p:p + 1, 0:512], -1.0, None, ALU.mult)
            s_bv.incd(v)
        block.vector(f_v)
        bv_var = s_bv.n

        def f_hop(sc):
            sc.wait_ge(s_bv.h, bv_var)
            for tc, p in ((0, 0), (1, 64)):
                s_bc.inc(sc.dma_start(rows_f[p:p + 1, 512:1024], rows_f[p + 32:p + 33, 0:512]), 16)
            sc.wait_ge(s_bc.h, s_bc.n)
        block.scalar(f_hop)
        bc_hop = s_bc.n

        def f_v1b(v):
            v.wait_ge(s_bc.h, bc_hop)
            for tc, p in ((0, 0), (1, 64)):
                v.tensor_mul(rows_f[p:p + 1, 1024:1536], rows_f[p:p + 1, 0:512], rows_f[p:p + 1, 0:512])
            v.drain()
            for tc, p in ((0, 0), (1, 64)):
                v.tensor_sub(rows_f[p:p + 1, 512:1024], rows_f[p:p + 1, 512:1024],
                             rows_f[p:p + 1, 1024:1536])
            s_bv.incd(v)
        block.vector(f_v1b)
        bv_var2 = s_bv.n

        def f_sc(sc):
            sc.wait_ge(s_bv.h, bv_var2)
            for tc, p in ((0, 0), (1, 64)):
                sc.activation(rows_f[p:p + 1, 1024:1536], rows_f[p:p + 1, 512:1024],
                              AF.Sqrt, bias=epsc[p:p + 1, :])
            s_bv.incd(sc)
        block.scalar(f_sc)
        bv_sig = s_bv.n

        def f_v2(v):
            v.wait_ge(s_bv.h, bv_sig)
            for tc, p in ((0, 0), (1, 64)):
                v.reciprocal(rows_f[p:p + 1, 1536:2048], rows_f[p:p + 1, 1024:1536])
            v.drain()
            sc1 = 0.5 if is_ffn else 1.0
            for tc, p in ((0, 0), (1, 64)):
                v.scalar_tensor_tensor(aux_bf[p:p + 1, 3072:3584],
                                       rows_f[p:p + 1, 1536:2048], sc1,
                                       aux_bf[p:p + 1, 2560:3072], ALU.mult, ALU.mult)
            s_bv.incd(v)
        block.vector(f_v2)
        bv_r = s_bv.n

        def f_d2a(sc):
            sc.wait_ge(s_bv.h, bv_var)
            if is_ffn:
                s_bc.inc(sc.dma_start(aux_bf[32:33, 3584:4096], aux_bf[0:1, 2560:3072]), 16)
                s_bc.inc(sc.dma_start(aux_bf[32:33, 4096:4608], aux_bf[64:65, 2560:3072]), 16)
            row = 64 if is_ffn else 96
            s_bc.inc(sc.dma_start(aux_bf[row:row + 1, 4096:4608], aux_bf[0:1, 2560:3072]), 16)
            s_bc.inc(sc.dma_start(aux_bf[row:row + 1, 4608:5120], aux_bf[64:65, 2560:3072]), 16)
            sc.wait_ge(s_bc.h, s_bc.n)
        block.scalar(f_d2a)
        bc_msl = s_bc.n

        def f_d2(sc):
            sc.wait_ge(s_bv.h, bv_r)
            s_bc.inc(sc.dma_start(dscr[0:1, 0:512], rows_f[0:1, 1536:2048]), 16)
            s_bc.inc(sc.dma_start(dscr[1:2, 0:512], rows_f[64:65, 1536:2048]), 16)
            s_bc.inc(sc.dma_start(dbr[0:1, :], aux_bf[0:1, 3072:3584]), 16)
            s_bc.inc(sc.dma_start(dbr[1:2, :], aux_bf[64:65, 3072:3584]), 16)
            sc.wait_ge(s_bc.h, s_bc.n)
            s_bc.inc(sc.dma_start(r_b[:, 0:512], dscr[0:1, 0:512].partition_broadcast(128)), 16)
            s_bc.inc(sc.dma_start(r_b[:, 512:1024], dscr[1:2, 0:512].partition_broadcast(128)), 16)
            s_bc.inc(sc.dma_start(rmb[:, 0:512], dbr[0:1, :].partition_broadcast(128)), 16)
            s_bc.inc(sc.dma_start(rmb[:, 512:1024], dbr[1:2, :].partition_broadcast(128)), 16)
            sc.wait_ge(s_bc.h, s_bc.n)
        block.scalar(f_d2)
        bc_done = s_bc.n

        if not is_ffn:
            return {"bc_done": bc_done, "bc_msl": bc_msl, "bv_end": bv_r, "bv_var": bv_var}

        def f_t(t):
            t.wait_ge(s_bc.h, bc_msl)
            for tc in range(2):
                msl = aux_bf[32:33, 3584 + tc * 512: 3584 + (tc + 1) * 512]
                t.matmul(pgate[32 * tc:32 * tc + 8, :], aux_bf[32:33, 128:136], msl,
                         start=False, stop=True, tile_position=(32, 32 * tc), skip_group_check=True)
                li = t.matmul(plow[tc][:], aux_bf[32:33, 0:128], msl,
                              start=False, stop=True, tile_position=(32, 0), skip_group_check=True)
            s_ax.inc(li)
        block.tensor(f_t)
        ax_mr = s_ax.n

        def f_v3(v):
            v.wait_ge(s_ax.h, ax_mr)
            v.wait_ge(s_bc.h, bc_done)
            v.tensor_mul(gstage[0:8, 0:512], pgate[0:8, :], r_b[0:8, 0:512])
            v.tensor_mul(gstage[32:40, 0:512], pgate[32:40, :], r_b[32:40, 512:1024])
            s_bv.incd(v)
        block.vector(f_v3)
        bv_gl = s_bv.n

        def f_t2(t):
            t.wait_ge(s_bv.h, bv_gl)
            for tt in range(8):
                half = tt // 4
                lhs = gstage[32 * half:32 * half + 8, (tt % 4) * 128:(tt % 4 + 1) * 128]
                rhs = idsb[32 * half:32 * half + 8, 32 * half:32 * half + 8]
                li = t.matmul(pstat[:, tt * 8:(tt + 1) * 8], lhs, rhs,
                              start=True, stop=True, tile_position=(32 * half, 0),
                              skip_group_check=True)
            s_ax.inc(li)
        block.tensor(f_t2)
        ax_tr = s_ax.n

        def f_v4(v):
            v.wait_ge(s_ax.h, ax_tr)
            v.tensor_copy(gv[:], pstat[:, 0:64])
            v.drain()
            for s in range(8):
                v.max(m8all[:, s * 8:(s + 1) * 8], gv[:, s * 8:(s + 1) * 8])
            v.drain()
            m3 = m8all[:].rearrange("p (s e) -> p s e", e=8)
            v.tensor_copy(gtmp1[:].rearrange("p (s e) -> p s e", e=8),
                          m3[:, :, 1:2].broadcast_to([128, 8, 8]))
            v.tensor_copy(gtmp2[:].rearrange("p (s e) -> p s e", e=8),
                          m3[:, :, 0:1].broadcast_to([128, 8, 8]))
            v.drain()
            v.tensor_tensor(gtmp3[:], gv[:], gtmp1[:], ALU.is_ge)
            v.tensor_sub(gtmp2[:], gv[:], gtmp2[:])
            s_bv.incd(v)
        block.vector(f_v4)
        bv_g1 = s_bv.n

        def f_sc2(sc):
            sc.wait_ge(s_bv.h, bv_g1)
            li = sc.activation(gtmp2[:], gtmp2[:], AF.Exp)
            s_bv.inc(li)
        block.scalar(f_sc2)
        bv_g2 = s_bv.n

        def f_v5(v):
            v.wait_ge(s_bv.h, bv_g2)
            v.tensor_mul(gtmp1[:], gtmp2[:], gtmp3[:])
            v.drain()
            v.reduce_sum(gtmp2[:, 0:8], gtmp1[:].rearrange("p (s e) -> p s e", e=8), axis=AX.X)
            v.drain()
            v.reciprocal(gtmp2[:, 8:16], gtmp2[:, 0:8])
            v.drain()
            rdex = gtmp2[:, 8:16].rearrange("p (s o) -> p s o", o=1).broadcast_to([128, 8, 8])
            v.tensor_copy(gtmp3[:].rearrange("p (s e) -> p s e", e=8), rdex)
            v.drain()
            v.scalar_tensor_tensor(wv[:], gtmp1[:], 2.0, gtmp3[:], ALU.mult, ALU.mult)
            s_bv.incd(v)
        block.vector(f_v5)
        bv_g3 = s_bv.n

        def f_t3(t):
            t.wait_ge(s_bv.h, bv_g3)
            for tt in range(8):
                row = 0 if tt < 4 else 32
                li = t.matmul(pgate[row:row + 8, (tt % 4) * 128:(tt % 4 + 1) * 128],
                              wv[:, tt * 8:(tt + 1) * 8], idsb[:, 0:128],
                              start=True, stop=True, tile_position=(0, row),
                              skip_group_check=True)
            s_ax.inc(li)
        block.tensor(f_t3)
        ax_wt = s_ax.n

        def f_v6(v):
            v.wait_ge(s_ax.h, ax_wt)
            v.tensor_copy(aux_bf[0:8, 0:512], pgate[0:8, :])
            v.drain()
            v.tensor_copy(aux_bf[0:8, 512:1024], pgate[32:40, :])
            s_bv.incd(v)
        block.vector(f_v6)
        bv_wf = s_bv.n

        def f_d3(sc):
            sc.wait_ge(s_bv.h, bv_wf)
            s_bc.inc(sc.dma_start(dbw, aux_bf[0:8, 0:1024]), 16)
            sc.wait_ge(s_bc.h, s_bc.n)
            for e in range(8):
                s_bc.inc(sc.dma_start(w2exp[e * 16:(e + 1) * 16, :],
                                      dbw[e:e + 1, :].partition_broadcast(16)), 16)
            sc.wait_ge(s_bc.h, s_bc.n)
        block.scalar(f_d3)
        bc_w2 = s_bc.n

        def f_v7(v):
            v.wait_ge(s_bc.h, bc_w2)
            for tc in range(2):
                v.tensor_mul(zs[:, tc * 512:(tc + 1) * 512], plow[tc][:],
                             w2exp[:, tc * 512:(tc + 1) * 512])
            s_bv.incd(v)
        block.vector(f_v7)
        return {"bc_done": bc_done, "bc_msl": bc_msl, "bv_end": s_bv.n, "bv_var": bv_var}

    def emit_sublayer(name, mode, chunk, xsrc, xdst, aux_next, jit_bsum=False,
                      mask_s1=False, x1_load=False, skip_boundary=False):
        is_ffn = (mode == "ffn")
        col_ready = {}
        prev_mm = s_mm.n
        ep_base = s_ep.n
        mm_base = s_mm.n
        wm_base = s_wm.n
        ws_base = s_ws.n
        scw_base = [s_scw2[0].n, s_scw2[1].n]
        out_base = [s_out2[0].n, s_out2[1].n]
        scw_tot = (s_scw2[0].n, s_scw2[1].n)

        cs_ax = {}

        def issue_col_dma(eng, nt):
            if nt < 2:
                eng.wait_ge(s_mm.h, prev_mm - 2 + 2 * nt if prev_mm >= 2 else prev_mm)
                eng.wait_ge(s_scw2[0].h, scw_tot[0])
                eng.wait_ge(s_scw2[1].h, scw_tot[1])
            if nt >= 2:
                eng.wait_ge(s_mm.h, mm_base + 2 * (nt - 1))
                if mode == "attn_jit":
                    eng.wait_ge(s_scw2[nt % 2].h, scw_base[nt % 2] + 16 * (nt // 2))
            dst = wcol[nt % 2]
            if mode == "ffn":
                src = baseT[nt]
            elif mode == "attn_jit":
                src = wT[nt]
            else:
                src = scrA[nt]
            wsem = s_wcs2[nt % 2] if mode == "attn_scr" else s_wc2[nt % 2]
            wsem.inc(eng.dma_start(dst[0:64, :], src[0:64]), 16)
            wsem.inc(eng.dma_start(dst[64:128, :], src[64:128]), 16)
            col_ready[nt] = (wsem.h, wsem.n)

        col_q = block.sync if mode == "attn_scr" else block.gpsimd

        def f_cols01(eng):
            issue_col_dma(eng, 0)
            issue_col_dma(eng, 1)
        col_q(f_cols01)
        barrier()

        jit = (mode == "attn_jit" or jit_bsum)
        rot = 3 if (mode == "attn_jit" or not ROT4) else 4

        def emit_k(g):
            nt, tc = g // 2, g % 2
            pm = pmain[g % rot]
            plain_attn = (not is_ffn and not jit)

            def f_k(t):
                if g < rot:
                    t.wait_ge(s_ep.h, ep_base)
                else:
                    t.wait_ge(s_ep.h, ep_base + g - (rot - 1))
                if mode == "attn_jit":
                    t.wait_ge(s_wm.h, wm_base + nt + 1)
                else:
                    t.wait_ge(*col_ready[nt])
                li = None
                for kt in range(KT):
                    li = t.matmul(pm[:], wcol_tile(wcol[nt % 2], kt), x_tile(xsrc, kt, tc),
                                  start=(kt == 0), stop=(kt == KT - 1 and plain_attn),
                                  skip_group_check=True)
                if plain_attn:
                    s_mm.inc(li)
            block.tensor(f_k)

        def emit_fin2(g, bnd=None):
            if not is_ffn and not jit:
                return
            nt, tc = g // 2, g % 2
            pm = pmain[g % rot]

            def f_fin(t):
                if bnd is not None:
                    if jit:
                        t.wait_ge(s_bc.h, bnd["bc_msl"])
                    if is_ffn:
                        t.wait_ge(s_bv.h, bnd["bv_end"])
                li = None
                if jit:
                    t.wait_ge(s_ws.h, ws_base + nt + 1)
                    mrow = 64 if is_ffn else 96
                    li = t.matmul(pm[:], aux_bf[mrow:mrow + 1, nt * 128:(nt + 1) * 128],
                                  aux_bf[mrow:mrow + 1, 4096 + tc * 512:4096 + (tc + 1) * 512],
                                  start=False, stop=(not is_ffn), tile_position=(mrow, 0),
                                  skip_group_check=True)
                if is_ffn:
                    li = t.matmul(pm[:], bcat_s[:, nt * 128:(nt + 1) * 128],
                                  zs[:, tc * 512:(tc + 1) * 512], start=False, stop=True,
                                  skip_group_check=True)
                s_mm.inc(li)
            block.tensor(f_fin)

        def emit_ep(g):
            nt, tc = g // 2, g % 2
            pm = pmain[g % rot]
            mmv = s_mm.n

            xpm = (wsum_pm if mode.startswith("attn") else bsum_pm)

            def f_ep(v):
                v.wait_ge(s_mm.h, mmv)
                if g == 0:
                    v.wait_ge(s_bc.h, bnd["bc_done"])
                sc1 = 0.5 if is_ffn else 1.0
                v.scalar_tensor_tensor(pm[:], pm[:], sc1, r_b[:, tc * 512:(tc + 1) * 512],
                                       ALU.mult, ALU.mult)
                v.drain()
                if not jit:
                    v.scalar_tensor_tensor(pm[:], rmb[:, tc * 512:(tc + 1) * 512],
                                           xpm[:, nt:nt + 1], pm[:], ALU.mult, ALU.add)
                    v.drain()
                bias_ap = (biasBh if is_ffn else biasA)[:, nt:nt + 1]
                if xdst is not None:
                    li = v.scalar_tensor_tensor(x_tile(xdst, nt, tc), pm[:], bias_ap,
                                                x_tile(xsrc, nt, tc), ALU.add, ALU.add)
                else:
                    v.wait_ge(s_out2[g % 2].h, out_base[g % 2] + 16 * (g // 2))
                    li = v.scalar_tensor_tensor(ostg[g % 2][:], pm[:], bias_ap,
                                                x_tile(xsrc, nt, tc), ALU.add, ALU.add)
                s_ep.inc(li)
            block.vector(f_ep)
            epv = s_ep.n

            if xdst is None:
                def f_out(sy):
                    sy.wait_ge(s_ep.h, epv)
                    dst = outT[nt * 128:(nt + 1) * 128,
                               chunk * TC + tc * 512: chunk * TC + (tc + 1) * 512]
                    s_out2[g % 2].inc(sy.dma_start(dst, ostg[g % 2][:]), 16)
                block.sync(f_out)

        if mode != "attn_jit":
            for g0 in range(2 if is_ffn else 0):
                emit_k(g0)

        bnd = emit_boundary_chain(is_ffn)

        aux_done = [0]

        def emit_aux_upto(upto):
            while aux_done[0] < upto:
                kt = aux_done[0]
                extra = bnd["bv_end"] if kt == 0 else None
                if aux_next[0] == "ep":
                    emit_aux_kt(kt, xdst, aux_next[1], s_ep.h, ep_base + 2 * kt + 2, extra)
                else:
                    sh, sv = xq_ready[kt // 4]
                    emit_aux_kt(kt, aux_next[2], aux_next[1], sh, sv, extra)
                aux_done[0] += 1

        for nt in range(NT):
            if x1_load and nt in (0, 2, 4, 6, 8, 10, 12, 14):
                q = nt // 2

                def f_x1(g, q=q):
                    src = xT_in[q * 4 * 128:(q + 1) * 4 * 128, TC:2 * TC]
                    src = src.rearrange("(kt p) t -> p kt t", p=128)
                    dst = xB[:, q * 4 * TC:(q + 1) * 4 * TC].rearrange("p (kt t) -> p kt t", t=TC)
                    sem = s_xq[q]
                    sem.inc(g.dma_start(dst, src), 16)
                    xq_ready[q] = (sem.h, sem.n)
                block.gpsimd(f_x1)

            if mode == "attn_jit":
                def f_vm(v, nt=nt):
                    v.wait_ge(*col_ready[nt])
                    if nt == 0:
                        v.wait_ge(s_kx.h, kexp_holder["ready"])
                    kexp = kexp_holder["kexp"]
                    src1 = kexp[:].rearrange("p (kt nb) -> p kt nb", nb=128)[:, :, 4 * nt:4 * nt + 4]
                    src1 = src1.rearrange("p kt (nb o) -> p kt nb o", o=1).broadcast_to([128, KT, 4, 32])
                    dst = wcol[nt % 2][:].rearrange("p (kt nb r) -> p kt nb r", nb=4, r=32)
                    li = v.tensor_tensor(dst, dst, src1, ALU.mult)
                    s_wm.inc(li)
                block.vector(f_vm)

                def f_scw(sy, nt=nt):
                    sy.wait_ge(s_wm.h, wm_base + nt + 1)
                    s_scw2[nt % 2].inc(sy.dma_start(scrA[nt], wcol[nt % 2][:]), 16)
                block.sync(f_scw)

            if mode == "attn_jit" or jit_bsum:
                row = 96 if mode == "attn_jit" else 64
                pcs = pmain[3] if mode == "attn_jit" else plow[0]
                xpm = wsum_pm if mode == "attn_jit" else bsum_pm

                pcs2 = pcs if nt % 2 == 0 else pgate

                def f_cs(t, nt=nt, row=row, pcs2=pcs2):
                    if nt == 0:
                        t.wait_ge(s_bv.h, bnd["bv_end"])
                    if mode == "attn_jit":
                        t.wait_ge(s_wm.h, wm_base + nt + 1)
                    else:
                        t.wait_ge(*col_ready[nt])
                    if nt > 1:
                        t.wait_ge(s_ws.h, ws_base + nt - 1)
                    for j in range(8):
                        li = t.matmul(pcs2[row:row + 1, :], ones[:],
                                      wcol[nt % 2][:, j * 512:(j + 1) * 512],
                                      start=(j == 0), stop=(j == 7),
                                      tile_position=(0, row), skip_group_check=True)
                    s_ax.inc(li)
                block.tensor(f_cs)
                ax_cs = s_ax.n
                cs_ax[nt] = ax_cs

                def f_csv(v, nt=nt, row=row, pcs2=pcs2):
                    v.wait_ge(s_ax.h, ax_cs)
                    src = pcs2[row:row + 1, :].rearrange("p (a n) -> p n a", a=4)
                    v.reduce_sum(rows_f[row:row + 1, 0:128], src, axis=AX.X)
                    v.drain()
                    li = v.tensor_copy(aux_bf[row:row + 1, nt * 128:(nt + 1) * 128],
                                       rows_f[row:row + 1, 0:128])
                    s_ws.inc(li)
                block.vector(f_csv)

            if mask_s1:
                red_hi = min(256, 16 + ((nt + 1) * 240 + 23) // 24)
                dma_hi = min(256, red_hi + 8)
                mm_hi = min(32, max(0, red_hi // 8 - 2))
                block.sync(lambda sy, h=dma_hi: f_mask_dma(sy, h))
                block.vector(lambda v, h=red_hi: f_mask_red(v, h))
                block.tensor(lambda t, h=mm_hi: f_mask_mm(t, h, bnd["bv_end"]))
                if nt == 25:
                    block.tensor(lambda t: f_mask_mm(t, 32, bnd["bv_end"]))

                    def f_bm_thr(v):
                        v.wait_ge(s_m1.h, 32)
                        s_bv.inc(v.tensor_scalar(bm01T[:], plow[1][:, 0:128], 0.5,
                                                 None, ALU.is_gt))
                    block.vector(f_bm_thr)
                    bv_thr = s_bv.n

                    def f_bm_tr(t):
                        t.wait_ge(s_bv.h, bv_thr)
                        s_m1.inc(t.matmul(plow[1][:, 128:256], bm01T[:], idsb[:, 0:128],
                                          start=True, stop=True, skip_group_check=True))
                    block.tensor(f_bm_tr)

                    def f_bm_ev(v):
                        v.wait_ge(s_m1.h, 33)
                        s_bv.inc(v.tensor_copy(bm01[:], plow[1][:, 128:256]))
                    block.vector(f_bm_ev)
                    bv_bm = s_bv.n

                    def f_kexp(sy):
                        sy.wait_ge(s_bv.h, bv_bm)
                        s_kx.inc(sy.dma_start(dbm, bm01[:]), 16)
                        sy.wait_ge(s_kx.h, s_kx.n)
                        kexp = kexp_holder["kexp"]
                        dbmv = dbm.rearrange("(kt j) n -> j kt n", j=4)
                        for j in range(4):
                            src = dbmv[j:j + 1].partition_broadcast(32)
                            s_kx.inc(sy.dma_start(kexp[32 * j:32 * j + 32, :], src), 16)
                        sy.wait_ge(s_kx.h, s_kx.n)
                    block.sync(f_kexp)
                    kexp_holder["ready"] = s_kx.n

            if nt + 2 < NT:
                col_q(lambda eng, n=nt + 2: issue_col_dma(eng, n))

            if aux_next is not None and nt >= 4:
                emit_aux_upto(nt - 3)

            if mode == "attn_jit" or not ROT4:
                if nt == 0:
                    if not is_ffn and mode == "attn_jit":
                        emit_k(0)
                        emit_fin2(0, bnd)
                        emit_ep(0)
                        emit_k(1)
                        emit_fin2(1, bnd)
                        emit_ep(1)
                    elif is_ffn:
                        emit_fin2(0, bnd)
                        emit_ep(0)
                        emit_fin2(1, bnd)
                        emit_ep(1)
                    else:
                        emit_k(0)
                        emit_fin2(0, bnd)
                        emit_ep(0)
                        emit_k(1)
                        emit_fin2(1, bnd)
                        emit_ep(1)
                else:
                    for tc in range(2):
                        g = 2 * nt + tc
                        emit_k(g)
                        emit_fin2(g)
                        emit_ep(g)
            elif nt == 0:
                if is_ffn:
                    emit_fin2(0, bnd)
                    emit_ep(0)
                    emit_fin2(1, bnd)
                    emit_ep(1)
                else:
                    emit_k(0)
                    emit_fin2(0, bnd)
                    emit_ep(0)
                    emit_k(1)
                    emit_fin2(1, bnd)
                    emit_ep(1)
            else:
                for tc in range(2):
                    g = 2 * nt + tc
                    emit_k(g)
                    emit_fin2(g)
                    emit_ep(g)

        if jit:
            row = 96 if mode == "attn_jit" else 64
            xpm2 = wsum_pm if mode == "attn_jit" else bsum_pm
            ptr = pmain[3] if mode == "attn_jit" else pgate

            def f_tr(t):
                t.wait_ge(s_ws.h, ws_base + NT)
                for nt2 in range(NT):
                    li = t.matmul(ptr[:, nt2:nt2 + 1],
                                  aux_bf[row:row + 1, nt2 * 128:(nt2 + 1) * 128],
                                  ones[row:row + 1, 0:1], start=True, stop=True,
                                  tile_position=(row, 0), skip_group_check=True)
                s_ax.inc(li)
            block.tensor(f_tr)
            axtr = s_ax.n

            def f_tre(v):
                v.wait_ge(s_ax.h, axtr)
                li = v.tensor_copy(xpm2[:, 0:NT], ptr[:, 0:NT])
                s_bv.inc(li)
            block.vector(f_tre)

        if aux_next is not None:
            emit_aux_upto(KT)

    def f_p0_sync(sy):
        s_ld.inc(sy.dma_start(idsb[:], ident.ap()), 16)
        s_ld.inc(sy.dma_start(ones[:], ones_c.ap()), 16)
        s_ld.inc(sy.dma_start(bpksb[:], bpk.ap()), 16)
        s_ld.inc(sy.dma_start(biasA[:], biasA_d), 16)
        s_ld.inc(sy.dma_start(biasB_t[:], biasB_d), 16)
        f_mask_dma(sy, 19)
    block.sync(f_p0_sync)
    block.vector(lambda v: f_mask_red(v, 16))

    def f_p0_g(g):
        s_ldg.inc(g.dma_start(gw_s[:].rearrange("p (kt e) -> p kt e", e=8),
                              gwT_d.rearrange("(kt p) e -> p kt e", p=128)), 16)
        s_ldg.inc(g.dma_start(acat_s[:].rearrange("p (kt e) -> p kt e", e=128),
                              acatT_d.rearrange("(kt p) e -> p kt e", p=128)), 16)
        s_ldg.inc(g.dma_start(bcat_s[:], bcat2_d), 16)
        for q in range(16):
            src = xT_in[q * 2 * 128:(q + 1) * 2 * 128, 0:TC]
            src = src.rearrange("(kt p) t -> p kt t", p=128)
            dst = xA[:, q * 2 * TC:(q + 1) * 2 * TC].rearrange("p (kt t) -> p kt t", t=TC)
            sem = s_xq[q]
            sem.inc(g.dma_start(dst, src), 16)
            xq_ready[q] = (sem.h, sem.n)
    block.gpsimd(f_p0_g)
    ld_all = s_ld.n
    ldg_all = s_ldg.n

    def f_p0_v(v):
        v.memset(epsc[:], EPS)
        v.wait_ge(s_ld.h, ld_all)
        v.wait_ge(s_ldg.h, ldg_all)
        li = v.tensor_scalar(biasBh[:], biasB_t[:], 0.5, None, ALU.mult)
        s_bv.inc(li)
    block.vector(f_p0_v)

    def f_p0_t(t):
        t.wait_ge(s_ld.h, ld_all)
        t.wait_ge(s_ldg.h, ldg_all)
        for kt in range(KT):
            t.matmul(pmain[0][32:33, 0:128], ones[:], acat_s[:, kt * 128:(kt + 1) * 128],
                     start=(kt == 0), stop=(kt == KT - 1), tile_position=(0, 32), skip_group_check=True)
            li = t.matmul(pmain[1][32:33, 0:8], ones[:], gw_s[:, kt * 8:(kt + 1) * 8],
                     start=(kt == 0), stop=(kt == KT - 1), tile_position=(0, 32), skip_group_check=True)
        s_ax.inc(li)
    block.tensor(f_p0_t)
    ax_p0 = s_ax.n

    def f_p0_v2(v):
        v.wait_ge(s_ax.h, ax_p0)
        v.tensor_copy(aux_bf[32:33, 0:128], pmain[0][32:33, 0:128])
        v.tensor_copy(aux_bf[32:33, 128:136], pmain[1][32:33, 0:8])
        s_bv.incd(v)
    block.vector(f_p0_v2)

    for kt in range(KT):
        emit_aux_kt(kt, xA, True, *xq_ready[kt // 2])

    emit_sublayer("s1c0", "ffn", 0, xA, xB, ("ep", False), jit_bsum=True, mask_s1=True)
    emit_sublayer("s2c0", "attn_jit", 0, xB, xA, ("ep", True))
    emit_sublayer("s3c0", "ffn", 0, xA, None, ("dma", True, xB), x1_load=True)

    emit_sublayer("s1c1", "ffn", 1, xB, xA, ("ep", False))
    emit_sublayer("s2c1", "attn_scr", 1, xA, xB, ("ep", True))
    emit_sublayer("s3c1", "ffn", 1, xB, None, None)

    o0, o1 = s_out2[0].n, s_out2[1].n

    def f_tail(sy):
        sy.wait_ge(s_out2[0].h, o0)
        sy.wait_ge(s_out2[1].h, o1)
    block.sync(f_tail)

    ctx.close()
    return nc


def kernel(L0_attn_data, L0_attn_mask, L0_attn_weight, L0_attn_bias,
           L0_ffn_gate_w, L0_ffn_base_w, L0_ffn_base_b, L0_ffn_lora_A,
           L0_ffn_lora_B):
    data = np.asarray(L0_attn_data, dtype=np.float32).reshape(16384, D)
    maskT_h = np.ascontiguousarray(np.asarray(L0_attn_mask, dtype=np.int32).T)
    def colblocks(w):
        wt = np.asarray(w, dtype=np.float32).T.reshape(KT, 128, NT, 128)
        return np.ascontiguousarray(wt.transpose(2, 1, 0, 3)).reshape(NT, 128, KT * 128)

    wT_h = colblocks(L0_attn_weight)
    baseT_h = colblocks(L0_ffn_base_w)
    gwT_h = np.ascontiguousarray(np.asarray(L0_ffn_gate_w, dtype=np.float32).T)
    acatT_h = np.ascontiguousarray(
        np.asarray(L0_ffn_lora_A, dtype=np.float32).reshape(128, D).T)
    bcat2_h = np.ascontiguousarray(
        np.asarray(L0_ffn_lora_B, dtype=np.float32).transpose(0, 2, 1).reshape(128, D))
    biasA_h = np.ascontiguousarray(np.asarray(L0_attn_bias, dtype=np.float32).reshape(32, 128).T)
    biasB_h = np.ascontiguousarray(np.asarray(L0_ffn_base_b, dtype=np.float32).reshape(32, 128).T)

    nc = build_program()
    in_maps = []
    for c in range(NCORES):
        xT = np.ascontiguousarray(data[c * TPC:(c + 1) * TPC, :].T)
        in_maps.append({
            "xT": xT, "maskT": maskT_h, "wT": wT_h, "baseT": baseT_h, "gwT": gwT_h,
            "acatT": acatT_h, "bcat2": bcat2_h, "biasA": biasA_h, "biasB": biasB_h,
        })
    res = run_bass_kernel_spmd(nc, in_maps, core_ids=list(range(NCORES)))
    parts = [res.results[c]["outT"].T for c in range(NCORES)]
    out = np.concatenate(parts, axis=0).reshape(2, 8192, D).astype(np.float32)
    return out


# version 20 (final)
# speedup vs baseline: 1.0753x; 1.0235x over previous
import numpy as np
import ml_dtypes
from contextlib import ExitStack

import concourse.bass as bass
import concourse.mybir as mybir
from concourse.bass_utils import run_bass_kernel_spmd

F32 = mybir.dt.float32
BF16 = mybir.dt.bfloat16
I32 = mybir.dt.int32
AF = mybir.ActivationFunctionType
ALU = mybir.AluOpType
AX = mybir.AxisListType
bfdt = ml_dtypes.bfloat16

ROT4 = True
D = 4096
KT = 32
NT = 32
TPC = 2048
TC = 1024
NCORES = 8
INV_D = 1.0 / D
EPS = 1e-5


class Sem:
    def __init__(self, h):
        self.h = h
        self.n = 0

    def inc(self, inst, by=1):
        inst.then_inc(self.h, by)
        self.n += by
        return self.n

    def incd(self, eng, by=1):
        eng.drain()
        eng.sem_inc(self.h, by)
        self.n += by
        return self.n


def build_program():
    nc = bass.Bass("TRN2", target_bir_lowering=False)

    xT_in = nc.dram_tensor("xT", [D, TPC], F32, kind="ExternalInput").ap()
    maskT = nc.dram_tensor("maskT", [D, D], I32, kind="ExternalInput").ap()
    wT = nc.dram_tensor("wT", [NT, 128, KT * 128], F32, kind="ExternalInput").ap()
    baseT = nc.dram_tensor("baseT", [NT, 128, KT * 128], F32, kind="ExternalInput").ap()
    gwT_d = nc.dram_tensor("gwT", [D, 8], F32, kind="ExternalInput").ap()
    acatT_d = nc.dram_tensor("acatT", [D, 128], F32, kind="ExternalInput").ap()
    bcat2_d = nc.dram_tensor("bcat2", [128, D], F32, kind="ExternalInput").ap()
    biasA_d = nc.dram_tensor("biasA", [128, 32], F32, kind="ExternalInput").ap()
    biasB_d = nc.dram_tensor("biasB", [128, 32], F32, kind="ExternalInput").ap()
    outT = nc.dram_tensor("outT", [D, TPC], F32, kind="ExternalOutput").ap()
    scrA = nc.dram_tensor("scrA", [NT, 128, KT * 128], BF16, kind="Internal").ap()
    dscr = nc.dram_tensor("dscr", [16, 1024], F32, kind="Internal").ap()
    dbm = nc.dram_tensor("dbm", [128, 128], BF16, kind="Internal").ap()
    dbw = nc.dram_tensor("dbw", [8, TC], BF16, kind="Internal").ap()
    dbr = nc.dram_tensor("dbr", [2, 512], BF16, kind="Internal").ap()

    ident = nc.inline_tensor(np.eye(128, dtype=bfdt), "ident")
    ones_c = nc.inline_tensor(np.ones((128, 1), dtype=bfdt), "ones_c")
    bpk_np = np.zeros((128, 4), dtype=bfdt)
    for p in range(128):
        bpk_np[p, p // 32] = 1.0
    bpk = nc.inline_tensor(bpk_np, "bpk")

    ctx = ExitStack()
    sb = lambda nm, sh, dt: ctx.enter_context(nc.sbuf_tensor(nm, sh, dt))

    xA = sb("xA", [128, KT * TC], BF16)
    xB = sb("xB", [128, KT * TC], BF16)
    wcol = [sb(f"wcol{i}", [128, KT * 128], BF16) for i in range(2)]
    acat_s = sb("acat_s", [128, KT * 128], BF16)
    bcat_s = sb("bcat_s", [128, D], BF16)
    gw_s = sb("gw_s", [128, KT * 8], BF16)
    idsb = sb("idsb", [128, 128], BF16)
    ones = sb("ones", [128, 1], BF16)
    bpksb = sb("bpksb", [128, 4], BF16)
    biasA = sb("biasA_s", [128, 32], F32)
    biasBh = sb("biasBh", [128, 32], F32)
    biasB_t = sb("biasB_t", [128, 32], F32)
    aux_bf = sb("aux_bf", [128, 5120], BF16)
    rows_f = sb("rows_f", [128, 2048], F32)
    r_b = sb("r_b", [128, TC], F32)
    w2exp = sb("w2exp", [128, TC], BF16)
    rmb = sb("rmb", [128, TC], BF16)
    bsum_pm = sb("bsum_pm", [128, 32], F32)
    wsum_pm = sb("wsum_pm", [128, 32], F32)
    zs = sb("zs", [128, TC], BF16)
    sqb = [sb(f"sqb{i}", [128, TC], BF16) for i in range(2)]
    gstage = sb("gstage", [128, 512], BF16)
    epsc = sb("epsc", [128, 1], F32)
    ostg = [sb(f"ostg{i}", [128, 512], F32) for i in range(2)]
    kexp = sb("kexp", [128, KT * 128], BF16)

    pmain = [ctx.enter_context(nc.psum_tensor(f"pm{i}", [128, 512], F32)) for i in range(4)]
    pstat = ctx.enter_context(nc.psum_tensor("pstat", [128, 512], F32))
    pgate = ctx.enter_context(nc.psum_tensor("pgate", [128, 512], F32))
    plow = [ctx.enter_context(nc.psum_tensor(f"pl{i}", [128, 512], F32)) for i in range(2)]

    names = ["s_wcs", "s_bc", "s_scw", "s_out", "s_ld", "s_ldg",
             "s_mm", "s_ep", "s_sq", "s_ax", "s_wm", "s_ws", "s_m1", "s_m1e", "s_bv"]
    S = {nm: Sem(ctx.enter_context(nc.semaphore(nm))) for nm in names}
    s_wcs, s_bc, s_scw, s_out, s_ld, s_ldg = (S[n] for n in names[:6])
    s_mm, s_ep, s_sq, s_ax, s_wm, s_ws, s_m1, s_m1e, s_bv = (S[n] for n in names[6:])
    s_xq = [Sem(ctx.enter_context(nc.semaphore(f"s_xq{i}"))) for i in range(16)]
    s_wc2 = [Sem(ctx.enter_context(nc.semaphore(f"s_wc{i}"))) for i in range(2)]
    s_mk4 = [Sem(ctx.enter_context(nc.semaphore(f"s_mk{i}"))) for i in range(4)]
    s_cvm = Sem(ctx.enter_context(nc.semaphore("s_cvm")))
    s_wcs2 = [Sem(ctx.enter_context(nc.semaphore(f"s_wcs{i}"))) for i in range(2)]
    s_scw2 = [Sem(ctx.enter_context(nc.semaphore(f"s_scw{i}"))) for i in range(2)]
    s_out2 = [Sem(ctx.enter_context(nc.semaphore(f"s_out{i}"))) for i in range(2)]
    s_kx = Sem(ctx.enter_context(nc.semaphore("s_kx")))
    s_cs2 = [Sem(ctx.enter_context(nc.semaphore(f"s_cs{i}"))) for i in range(2)]
    s_mc = Sem(ctx.enter_context(nc.semaphore("s_mc")))
    xq_ready = [None] * 16
    s_bar = Sem(ctx.enter_context(nc.semaphore("s_bar")))
    bar_k = [0]

    def barrier():
        bar_k[0] += 1
        tgt = 5 * bar_k[0]

        def mk(eng):
            eng.drain()
            eng.sem_inc(s_bar.h, 1)
            eng.wait_ge(s_bar.h, tgt)
        block.tensor(mk)
        block.vector(mk)
        block.scalar(mk)
        block.gpsimd(mk)
        block.sync(mk)
        s_bar.n = tgt

    block = ctx.enter_context(nc.Block())

    mskraw = [kexp[:, i * 1024:(i + 1) * 1024].bitcast(I32) for i in range(3)]
    knb_f = kexp[:, 3072:3584].bitcast(F32)
    knb_b = kexp[:, 3584:3968]
    bm01T = kexp[:, 3968:4096]
    gv = rows_f[:, 512:576]
    wvrep = aux_bf[:, 0:1024]
    m8all = rows_f[:, 576:640]
    gtmp1 = rows_f[:, 640:704]
    gtmp2 = rows_f[:, 704:768]
    gtmp3 = rows_f[:, 768:832]
    wv = rows_f[:, 832:864].bitcast(BF16)
    bm01 = sb("bm01", [128, 128], BF16)

    kexp_holder = {"kexp": kexp}
    aux_guard = [0, 0]

    def x_tile(xbuf, kt, tc=None):
        if tc is None:
            return xbuf[:, kt * TC:(kt + 1) * TC]
        return xbuf[:, kt * TC + tc * 512: kt * TC + (tc + 1) * 512]

    def wcol_tile(buf, kt):
        return buf[:, kt * 128:(kt + 1) * 128]

    msk = {"dma": 0, "red": 0, "mm": 0}

    def f_mask_dma(sy, hi):
        while msk["dma"] < hi:
            s = msk["dma"]
            kt, ncn = divmod(s, 8)
            if s >= 3:
                sy.wait_ge(s_mc.h, s - 2)
            s_mk4[s % 3].inc(sy.dma_start(
                mskraw[s % 3],
                maskT[kt * 128:(kt + 1) * 128, ncn * 512:(ncn + 1) * 512]), 16)
            msk["dma"] += 1

    def f_mask_red(v, hi):
        while msk["red"] < hi:
            s = msk["red"]
            kt, ncn = divmod(s, 8)
            b = s % 3
            v.wait_ge(s_mk4[b].h, 16 * (s // 3 + 1))
            src = mskraw[b].rearrange("p (nb w) -> p nb w", w=32)
            s_mc.inc(v.reduce_sum(
                knb_f[:, (kt % 2) * 128 + ncn * 16: (kt % 2) * 128 + (ncn + 1) * 16],
                src, axis=AX.X))
            if ncn == 7:
                if kt >= 3:
                    v.wait_ge(s_m1.h, kt - 2)
                v.drain()
                s_cvm.inc(v.tensor_copy(
                    knb_b[:, (kt % 3) * 128:(kt % 3 + 1) * 128],
                    knb_f[:, (kt % 2) * 128:(kt % 2 + 1) * 128]))
            msk["red"] += 1

    def f_mask_mm(t, hi, bv_gate):
        while msk["mm"] < hi:
            kt = msk["mm"]
            if kt == 0:
                t.wait_ge(s_bv.h, bv_gate)
            t.wait_ge(s_cvm.h, kt + 1)
            s_m1.inc(t.matmul(plow[1][:, 4 * kt:4 * kt + 4],
                              knb_b[:, (kt % 3) * 128:(kt % 3 + 1) * 128],
                              bpksb[:], start=True, stop=True, skip_group_check=True))
            msk["mm"] += 1

    def emit_aux_kt(kt, xbuf, do_gl, src_sem_h, src_val, extra_bv=None):
        guard = aux_guard[kt % 2]
        sq = sqb[kt % 2]

        def f_sc(sc):
            sc.wait_ge(src_sem_h, src_val)
            if guard:
                sc.wait_ge(s_ax.h, guard)
            s_sq.inc(sc.activation(sq[:], x_tile(xbuf, kt), AF.Square))
        block.scalar(f_sc)
        sqv = s_sq.n

        def f_t(t):
            t.wait_ge(s_sq.h, sqv)
            if extra_bv is not None:
                t.wait_ge(s_bv.h, extra_bv)
            last = (kt == KT - 1)
            for tc in range(2):
                li = t.matmul(pstat[64 * tc:64 * tc + 1, :], ones[:], x_tile(xbuf, kt, tc),
                              start=(kt == 0), stop=last,
                              tile_position=(0, 64 * tc), skip_group_check=True)
                li = t.matmul(pstat[64 * tc + 32:64 * tc + 33, :], ones[:],
                              sq[:, tc * 512:(tc + 1) * 512],
                              start=(kt == 0), stop=last, tile_position=(0, 64 * tc + 32), skip_group_check=True)
                if do_gl:
                    li = t.matmul(pgate[32 * tc:32 * tc + 8, :], gw_s[:, kt * 8:(kt + 1) * 8],
                                  x_tile(xbuf, kt, tc), start=(kt == 0),
                                  stop=False, tile_position=(0, 32 * tc), skip_group_check=True)
                    li = t.matmul(plow[tc][:], acat_s[:, kt * 128:(kt + 1) * 128],
                             x_tile(xbuf, kt, tc), start=(kt == 0), stop=False, skip_group_check=True)
            s_ax.inc(li)
        block.tensor(f_t)
        aux_guard[kt % 2] = s_ax.n

    def emit_boundary_chain(is_ffn):
        def f_v(v):
            for tc, p in ((0, 0), (1, 64)):
                v.tensor_scalar(rows_f[p:p + 1, 0:512], pstat[p:p + 1, :], INV_D, None, ALU.mult)
                v.tensor_scalar(rows_f[p + 32:p + 33, 0:512], pstat[p + 32:p + 33, :], INV_D, None, ALU.mult)
            v.drain()
            for tc, p in ((0, 0), (1, 64)):
                v.tensor_scalar(aux_bf[p:p + 1, 2560:3072], rows_f[p:p + 1, 0:512], -1.0, None, ALU.mult)
            s_bv.incd(v)
        block.vector(f_v)
        bv_var = s_bv.n

        def f_hop(sc):
            sc.wait_ge(s_bv.h, bv_var)
            for tc, p in ((0, 0), (1, 64)):
                s_bc.inc(sc.dma_start(rows_f[p:p + 1, 512:1024], rows_f[p + 32:p + 33, 0:512]), 16)
            sc.wait_ge(s_bc.h, s_bc.n)
        block.scalar(f_hop)
        bc_hop = s_bc.n

        def f_v1b(v):
            v.wait_ge(s_bc.h, bc_hop)
            for tc, p in ((0, 0), (1, 64)):
                v.tensor_mul(rows_f[p:p + 1, 1024:1536], rows_f[p:p + 1, 0:512], rows_f[p:p + 1, 0:512])
            v.drain()
            for tc, p in ((0, 0), (1, 64)):
                v.tensor_sub(rows_f[p:p + 1, 512:1024], rows_f[p:p + 1, 512:1024],
                             rows_f[p:p + 1, 1024:1536])
            s_bv.incd(v)
        block.vector(f_v1b)
        bv_var2 = s_bv.n

        def f_sc(sc):
            sc.wait_ge(s_bv.h, bv_var2)
            for tc, p in ((0, 0), (1, 64)):
                sc.activation(rows_f[p:p + 1, 1024:1536], rows_f[p:p + 1, 512:1024],
                              AF.Sqrt, bias=epsc[p:p + 1, :])
            s_bv.incd(sc)
        block.scalar(f_sc)
        bv_sig = s_bv.n

        def f_v2(v):
            v.wait_ge(s_bv.h, bv_sig)
            for tc, p in ((0, 0), (1, 64)):
                v.reciprocal(rows_f[p:p + 1, 1536:2048], rows_f[p:p + 1, 1024:1536])
            v.drain()
            sc1 = 0.5 if is_ffn else 1.0
            for tc, p in ((0, 0), (1, 64)):
                v.scalar_tensor_tensor(aux_bf[p:p + 1, 3072:3584],
                                       rows_f[p:p + 1, 1536:2048], sc1,
                                       aux_bf[p:p + 1, 2560:3072], ALU.mult, ALU.mult)
            s_bv.incd(v)
        block.vector(f_v2)
        bv_r = s_bv.n

        def f_d2a(sc):
            sc.wait_ge(s_bv.h, bv_var)
            if is_ffn:
                s_bc.inc(sc.dma_start(aux_bf[32:33, 3584:4096], aux_bf[0:1, 2560:3072]), 16)
                s_bc.inc(sc.dma_start(aux_bf[32:33, 4096:4608], aux_bf[64:65, 2560:3072]), 16)
            row = 64 if is_ffn else 96
            s_bc.inc(sc.dma_start(aux_bf[row:row + 1, 4096:4608], aux_bf[0:1, 2560:3072]), 16)
            s_bc.inc(sc.dma_start(aux_bf[row:row + 1, 4608:5120], aux_bf[64:65, 2560:3072]), 16)
            sc.wait_ge(s_bc.h, s_bc.n)
        block.scalar(f_d2a)
        bc_msl = s_bc.n

        def f_d2(sc):
            sc.wait_ge(s_bv.h, bv_r)
            s_bc.inc(sc.dma_start(dscr[0:1, 0:512], rows_f[0:1, 1536:2048]), 16)
            s_bc.inc(sc.dma_start(dscr[1:2, 0:512], rows_f[64:65, 1536:2048]), 16)
            s_bc.inc(sc.dma_start(dbr[0:1, :], aux_bf[0:1, 3072:3584]), 16)
            s_bc.inc(sc.dma_start(dbr[1:2, :], aux_bf[64:65, 3072:3584]), 16)
            sc.wait_ge(s_bc.h, s_bc.n)
            s_bc.inc(sc.dma_start(r_b[:, 0:512], dscr[0:1, 0:512].partition_broadcast(128)), 16)
            s_bc.inc(sc.dma_start(r_b[:, 512:1024], dscr[1:2, 0:512].partition_broadcast(128)), 16)
            s_bc.inc(sc.dma_start(rmb[:, 0:512], dbr[0:1, :].partition_broadcast(128)), 16)
            s_bc.inc(sc.dma_start(rmb[:, 512:1024], dbr[1:2, :].partition_broadcast(128)), 16)
            sc.wait_ge(s_bc.h, s_bc.n)
        block.scalar(f_d2)
        bc_done = s_bc.n

        if not is_ffn:
            return {"bc_done": bc_done, "bc_msl": bc_msl, "bv_end": bv_r, "bv_var": bv_var}

        def f_t(t):
            t.wait_ge(s_bc.h, bc_msl)
            for tc in range(2):
                msl = aux_bf[32:33, 3584 + tc * 512: 3584 + (tc + 1) * 512]
                t.matmul(pgate[32 * tc:32 * tc + 8, :], aux_bf[32:33, 128:136], msl,
                         start=False, stop=True, tile_position=(32, 32 * tc), skip_group_check=True)
                li = t.matmul(plow[tc][:], aux_bf[32:33, 0:128], msl,
                              start=False, stop=True, tile_position=(32, 0), skip_group_check=True)
            s_ax.inc(li)
        block.tensor(f_t)
        ax_mr = s_ax.n

        def f_v3(v):
            v.wait_ge(s_ax.h, ax_mr)
            v.wait_ge(s_bc.h, bc_done)
            v.tensor_mul(gstage[0:8, 0:512], pgate[0:8, :], r_b[0:8, 0:512])
            v.tensor_mul(gstage[32:40, 0:512], pgate[32:40, :], r_b[32:40, 512:1024])
            s_bv.incd(v)
        block.vector(f_v3)
        bv_gl = s_bv.n

        def f_t2(t):
            t.wait_ge(s_bv.h, bv_gl)
            for tt in range(8):
                half = tt // 4
                lhs = gstage[32 * half:32 * half + 8, (tt % 4) * 128:(tt % 4 + 1) * 128]
                rhs = idsb[32 * half:32 * half + 8, 32 * half:32 * half + 8]
                li = t.matmul(pstat[:, tt * 8:(tt + 1) * 8], lhs, rhs,
                              start=True, stop=True, tile_position=(32 * half, 0),
                              skip_group_check=True)
            s_ax.inc(li)
        block.tensor(f_t2)
        ax_tr = s_ax.n

        def f_v4(v):
            v.wait_ge(s_ax.h, ax_tr)
            v.tensor_copy(gv[:], pstat[:, 0:64])
            v.drain()
            for s in range(8):
                v.max(m8all[:, s * 8:(s + 1) * 8], gv[:, s * 8:(s + 1) * 8])
            v.drain()
            m3 = m8all[:].rearrange("p (s e) -> p s e", e=8)
            v.tensor_copy(gtmp1[:].rearrange("p (s e) -> p s e", e=8),
                          m3[:, :, 1:2].broadcast_to([128, 8, 8]))
            v.tensor_copy(gtmp2[:].rearrange("p (s e) -> p s e", e=8),
                          m3[:, :, 0:1].broadcast_to([128, 8, 8]))
            v.drain()
            v.tensor_tensor(gtmp3[:], gv[:], gtmp1[:], ALU.is_ge)
            v.tensor_sub(gtmp2[:], gv[:], gtmp2[:])
            s_bv.incd(v)
        block.vector(f_v4)
        bv_g1 = s_bv.n

        def f_sc2(sc):
            sc.wait_ge(s_bv.h, bv_g1)
            li = sc.activation(gtmp2[:], gtmp2[:], AF.Exp)
            s_bv.inc(li)
        block.scalar(f_sc2)
        bv_g2 = s_bv.n

        def f_v5(v):
            v.wait_ge(s_bv.h, bv_g2)
            v.tensor_mul(gtmp1[:], gtmp2[:], gtmp3[:])
            v.drain()
            v.reduce_sum(gtmp2[:, 0:8], gtmp1[:].rearrange("p (s e) -> p s e", e=8), axis=AX.X)
            v.drain()
            v.reciprocal(gtmp2[:, 8:16], gtmp2[:, 0:8])
            v.drain()
            rdex = gtmp2[:, 8:16].rearrange("p (s o) -> p s o", o=1).broadcast_to([128, 8, 8])
            v.tensor_copy(gtmp3[:].rearrange("p (s e) -> p s e", e=8), rdex)
            v.drain()
            v.scalar_tensor_tensor(wv[:], gtmp1[:], 2.0, gtmp3[:], ALU.mult, ALU.mult)
            s_bv.incd(v)
        block.vector(f_v5)
        bv_g3 = s_bv.n

        def f_t3(t):
            t.wait_ge(s_bv.h, bv_g3)
            for tt in range(8):
                row = 0 if tt < 4 else 32
                li = t.matmul(pgate[row:row + 8, (tt % 4) * 128:(tt % 4 + 1) * 128],
                              wv[:, tt * 8:(tt + 1) * 8], idsb[:, 0:128],
                              start=True, stop=True, tile_position=(0, row),
                              skip_group_check=True)
            s_ax.inc(li)
        block.tensor(f_t3)
        ax_wt = s_ax.n

        def f_v6(v):
            v.wait_ge(s_ax.h, ax_wt)
            v.tensor_copy(aux_bf[0:8, 0:512], pgate[0:8, :])
            v.drain()
            v.tensor_copy(aux_bf[0:8, 512:1024], pgate[32:40, :])
            s_bv.incd(v)
        block.vector(f_v6)
        bv_wf = s_bv.n

        def f_d3(sc):
            sc.wait_ge(s_bv.h, bv_wf)
            s_bc.inc(sc.dma_start(dbw, aux_bf[0:8, 0:1024]), 16)
            sc.wait_ge(s_bc.h, s_bc.n)
            for e in range(8):
                s_bc.inc(sc.dma_start(w2exp[e * 16:(e + 1) * 16, :],
                                      dbw[e:e + 1, :].partition_broadcast(16)), 16)
            sc.wait_ge(s_bc.h, s_bc.n)
        block.scalar(f_d3)
        bc_w2 = s_bc.n

        def f_v7(v):
            v.wait_ge(s_bc.h, bc_w2)
            for tc in range(2):
                v.tensor_mul(zs[:, tc * 512:(tc + 1) * 512], plow[tc][:],
                             w2exp[:, tc * 512:(tc + 1) * 512])
            s_bv.incd(v)
        block.vector(f_v7)
        return {"bc_done": bc_done, "bc_msl": bc_msl, "bv_end": s_bv.n, "bv_var": bv_var}

    def emit_sublayer(name, mode, chunk, xsrc, xdst, aux_next, jit_bsum=False,
                      mask_s1=False, x1_load=False, skip_boundary=False):
        is_ffn = (mode == "ffn")
        col_ready = {}
        prev_mm = s_mm.n
        ep_base = s_ep.n
        mm_base = s_mm.n
        wm_base = s_wm.n
        ws_base = s_ws.n
        scw_base = [s_scw2[0].n, s_scw2[1].n]
        out_base = [s_out2[0].n, s_out2[1].n]
        scw_tot = (s_scw2[0].n, s_scw2[1].n)

        cs_ax = {}

        def issue_col_dma(eng, nt):
            if nt < 2:
                eng.wait_ge(s_mm.h, prev_mm - 2 + 2 * nt if prev_mm >= 2 else prev_mm)
                eng.wait_ge(s_scw2[0].h, scw_tot[0])
                eng.wait_ge(s_scw2[1].h, scw_tot[1])
            if nt >= 2:
                eng.wait_ge(s_mm.h, mm_base + 2 * (nt - 1))
                if mode == "attn_jit":
                    eng.wait_ge(s_scw2[nt % 2].h, scw_base[nt % 2] + 16 * (nt // 2))
            dst = wcol[nt % 2]
            if mode == "ffn":
                src = baseT[nt]
            elif mode == "attn_jit":
                src = wT[nt]
            else:
                src = scrA[nt]
            wsem = s_wcs2[nt % 2] if mode == "attn_scr" else s_wc2[nt % 2]
            wsem.inc(eng.dma_start(dst[0:64, :], src[0:64]), 16)
            wsem.inc(eng.dma_start(dst[64:128, :], src[64:128]), 16)
            col_ready[nt] = (wsem.h, wsem.n)

        col_q = block.sync if mode == "attn_scr" else block.gpsimd

        def f_cols01(eng):
            issue_col_dma(eng, 0)
            issue_col_dma(eng, 1)
        col_q(f_cols01)
        barrier()

        jit = (mode == "attn_jit" or jit_bsum)
        rot = 3 if (mode == "attn_jit" or not ROT4) else 4

        def emit_k(g):
            nt, tc = g // 2, g % 2
            pm = pmain[g % rot]
            plain_attn = (not is_ffn and not jit)

            def f_k(t):
                if g < rot:
                    t.wait_ge(s_ep.h, ep_base)
                else:
                    t.wait_ge(s_ep.h, ep_base + g - (rot - 1))
                if mode == "attn_jit":
                    t.wait_ge(s_wm.h, wm_base + nt + 1)
                else:
                    t.wait_ge(*col_ready[nt])
                li = None
                for kt in range(KT):
                    li = t.matmul(pm[:], wcol_tile(wcol[nt % 2], kt), x_tile(xsrc, kt, tc),
                                  start=(kt == 0), stop=(kt == KT - 1 and plain_attn),
                                  skip_group_check=True)
                if plain_attn:
                    s_mm.inc(li)
            block.tensor(f_k)

        def emit_fin2(g, bnd=None):
            if not is_ffn and not jit:
                return
            nt, tc = g // 2, g % 2
            pm = pmain[g % rot]

            def f_fin(t):
                if bnd is not None:
                    if jit:
                        t.wait_ge(s_bc.h, bnd["bc_msl"])
                    if is_ffn:
                        t.wait_ge(s_bv.h, bnd["bv_end"])
                li = None
                if jit:
                    t.wait_ge(s_ws.h, ws_base + nt + 1)
                    mrow = 64 if is_ffn else 96
                    li = t.matmul(pm[:], aux_bf[mrow:mrow + 1, nt * 128:(nt + 1) * 128],
                                  aux_bf[mrow:mrow + 1, 4096 + tc * 512:4096 + (tc + 1) * 512],
                                  start=False, stop=(not is_ffn), tile_position=(mrow, 0),
                                  skip_group_check=True)
                if is_ffn:
                    li = t.matmul(pm[:], bcat_s[:, nt * 128:(nt + 1) * 128],
                                  zs[:, tc * 512:(tc + 1) * 512], start=False, stop=True,
                                  skip_group_check=True)
                s_mm.inc(li)
            block.tensor(f_fin)

        def emit_ep(g):
            nt, tc = g // 2, g % 2
            pm = pmain[g % rot]
            mmv = s_mm.n

            xpm = (wsum_pm if mode.startswith("attn") else bsum_pm)

            def f_ep(v):
                v.wait_ge(s_mm.h, mmv)
                if g == 0:
                    v.wait_ge(s_bc.h, bnd["bc_done"])
                sc1 = 0.5 if is_ffn else 1.0
                v.scalar_tensor_tensor(pm[:], pm[:], sc1, r_b[:, tc * 512:(tc + 1) * 512],
                                       ALU.mult, ALU.mult)
                v.drain()
                if not jit:
                    v.scalar_tensor_tensor(pm[:], rmb[:, tc * 512:(tc + 1) * 512],
                                           xpm[:, nt:nt + 1], pm[:], ALU.mult, ALU.add)
                    v.drain()
                bias_ap = (biasBh if is_ffn else biasA)[:, nt:nt + 1]
                if xdst is not None:
                    li = v.scalar_tensor_tensor(x_tile(xdst, nt, tc), pm[:], bias_ap,
                                                x_tile(xsrc, nt, tc), ALU.add, ALU.add)
                else:
                    v.wait_ge(s_out2[g % 2].h, out_base[g % 2] + 16 * (g // 2))
                    li = v.scalar_tensor_tensor(ostg[g % 2][:], pm[:], bias_ap,
                                                x_tile(xsrc, nt, tc), ALU.add, ALU.add)
                s_ep.inc(li)
            block.vector(f_ep)
            epv = s_ep.n

            if xdst is None:
                def f_out(sy):
                    sy.wait_ge(s_ep.h, epv)
                    dst = outT[nt * 128:(nt + 1) * 128,
                               chunk * TC + tc * 512: chunk * TC + (tc + 1) * 512]
                    s_out2[g % 2].inc(sy.dma_start(dst, ostg[g % 2][:]), 16)
                block.sync(f_out)

        if mode != "attn_jit":
            for g0 in range(2 if is_ffn else 0):
                emit_k(g0)

        bnd = emit_boundary_chain(is_ffn)

        aux_done = [0]

        def emit_aux_upto(upto):
            while aux_done[0] < upto:
                kt = aux_done[0]
                extra = bnd["bv_end"] if kt == 0 else None
                if aux_next[0] == "ep":
                    emit_aux_kt(kt, xdst, aux_next[1], s_ep.h, ep_base + 2 * kt + 2, extra)
                else:
                    sh, sv = xq_ready[kt // 4]
                    emit_aux_kt(kt, aux_next[2], aux_next[1], sh, sv, extra)
                aux_done[0] += 1

        for nt in range(NT):
            if x1_load and nt in (0, 2, 4, 6, 8, 10, 12, 14):
                q = nt // 2

                def f_x1(g, q=q):
                    src = xT_in[q * 4 * 128:(q + 1) * 4 * 128, TC:2 * TC]
                    src = src.rearrange("(kt p) t -> p kt t", p=128)
                    dst = xB[:, q * 4 * TC:(q + 1) * 4 * TC].rearrange("p (kt t) -> p kt t", t=TC)
                    sem = s_xq[q]
                    sem.inc(g.dma_start(dst, src), 16)
                    xq_ready[q] = (sem.h, sem.n)
                block.gpsimd(f_x1)

            if mode == "attn_jit":
                def f_vm(v, nt=nt):
                    v.wait_ge(*col_ready[nt])
                    if nt == 0:
                        v.wait_ge(s_kx.h, kexp_holder["ready"])
                    kexp = kexp_holder["kexp"]
                    src1 = kexp[:].rearrange("p (kt nb) -> p kt nb", nb=128)[:, :, 4 * nt:4 * nt + 4]
                    src1 = src1.rearrange("p kt (nb o) -> p kt nb o", o=1).broadcast_to([128, KT, 4, 32])
                    dst = wcol[nt % 2][:].rearrange("p (kt nb r) -> p kt nb r", nb=4, r=32)
                    li = v.tensor_tensor(dst, dst, src1, ALU.mult)
                    s_wm.inc(li)
                block.vector(f_vm)

                def f_scw(sy, nt=nt):
                    sy.wait_ge(s_wm.h, wm_base + nt + 1)
                    s_scw2[nt % 2].inc(sy.dma_start(scrA[nt], wcol[nt % 2][:]), 16)
                block.sync(f_scw)

            if mode == "attn_jit" or jit_bsum:
                row = 96 if mode == "attn_jit" else 64
                pcs = pmain[3] if mode == "attn_jit" else plow[0]
                xpm = wsum_pm if mode == "attn_jit" else bsum_pm

                pcs2 = pcs if nt % 2 == 0 else pgate

                def f_cs(t, nt=nt, row=row, pcs2=pcs2):
                    if nt == 0:
                        t.wait_ge(s_bv.h, bnd["bv_end"])
                    if mode == "attn_jit":
                        t.wait_ge(s_wm.h, wm_base + nt + 1)
                    else:
                        t.wait_ge(*col_ready[nt])
                    if nt > 1:
                        t.wait_ge(s_ws.h, ws_base + nt - 1)
                    for j in range(8):
                        li = t.matmul(pcs2[row:row + 1, :], ones[:],
                                      wcol[nt % 2][:, j * 512:(j + 1) * 512],
                                      start=(j == 0), stop=(j == 7),
                                      tile_position=(0, row), skip_group_check=True)
                    s_ax.inc(li)
                block.tensor(f_cs)
                ax_cs = s_ax.n
                cs_ax[nt] = ax_cs

                def f_csv(v, nt=nt, row=row, pcs2=pcs2):
                    v.wait_ge(s_ax.h, ax_cs)
                    src = pcs2[row:row + 1, :].rearrange("p (a n) -> p n a", a=4)
                    v.reduce_sum(rows_f[row:row + 1, 0:128], src, axis=AX.X)
                    v.drain()
                    li = v.tensor_copy(aux_bf[row:row + 1, nt * 128:(nt + 1) * 128],
                                       rows_f[row:row + 1, 0:128])
                    s_ws.inc(li)
                block.vector(f_csv)

            if mask_s1:
                red_hi = min(256, 16 + ((nt + 1) * 240 + 27) // 28)
                dma_hi = min(256, red_hi + 8)
                mm_hi = min(32, max(0, red_hi // 8 - 2))
                block.sync(lambda sy, h=dma_hi: f_mask_dma(sy, h))
                block.vector(lambda v, h=red_hi: f_mask_red(v, h))
                block.tensor(lambda t, h=mm_hi: f_mask_mm(t, h, bnd["bv_end"]))
                if nt == 29:
                    block.tensor(lambda t: f_mask_mm(t, 32, bnd["bv_end"]))

                    def f_bm_thr(v):
                        v.wait_ge(s_m1.h, 32)
                        s_bv.inc(v.tensor_scalar(bm01T[:], plow[1][:, 0:128], 0.5,
                                                 None, ALU.is_gt))
                    block.vector(f_bm_thr)
                    bv_thr = s_bv.n

                    def f_bm_tr(t):
                        t.wait_ge(s_bv.h, bv_thr)
                        s_m1.inc(t.matmul(plow[1][:, 128:256], bm01T[:], idsb[:, 0:128],
                                          start=True, stop=True, skip_group_check=True))
                    block.tensor(f_bm_tr)

                    def f_bm_ev(v):
                        v.wait_ge(s_m1.h, 33)
                        s_bv.inc(v.tensor_copy(bm01[:], plow[1][:, 128:256]))
                    block.vector(f_bm_ev)
                    bv_bm = s_bv.n

                    def f_kexp(sy):
                        sy.wait_ge(s_bv.h, bv_bm)
                        s_kx.inc(sy.dma_start(dbm, bm01[:]), 16)
                        sy.wait_ge(s_kx.h, s_kx.n)
                        kexp = kexp_holder["kexp"]
                        dbmv = dbm.rearrange("(kt j) n -> j kt n", j=4)
                        for j in range(4):
                            src = dbmv[j:j + 1].partition_broadcast(32)
                            s_kx.inc(sy.dma_start(kexp[32 * j:32 * j + 32, :], src), 16)
                        sy.wait_ge(s_kx.h, s_kx.n)
                    block.sync(f_kexp)
                    kexp_holder["ready"] = s_kx.n

            if nt + 2 < NT:
                col_q(lambda eng, n=nt + 2: issue_col_dma(eng, n))

            if aux_next is not None and nt >= 3:
                emit_aux_upto(nt - 2)

            if mode == "attn_jit" or not ROT4:
                if nt == 0:
                    if not is_ffn and mode == "attn_jit":
                        emit_k(0)
                        emit_fin2(0, bnd)
                        emit_ep(0)
                        emit_k(1)
                        emit_fin2(1, bnd)
                        emit_ep(1)
                    elif is_ffn:
                        emit_fin2(0, bnd)
                        emit_ep(0)
                        emit_fin2(1, bnd)
                        emit_ep(1)
                    else:
                        emit_k(0)
                        emit_fin2(0, bnd)
                        emit_ep(0)
                        emit_k(1)
                        emit_fin2(1, bnd)
                        emit_ep(1)
                else:
                    for tc in range(2):
                        g = 2 * nt + tc
                        emit_k(g)
                        emit_fin2(g)
                        emit_ep(g)
            elif nt == 0:
                if is_ffn:
                    emit_fin2(0, bnd)
                    emit_ep(0)
                    emit_fin2(1, bnd)
                    emit_ep(1)
                else:
                    emit_k(0)
                    emit_fin2(0, bnd)
                    emit_ep(0)
                    emit_k(1)
                    emit_fin2(1, bnd)
                    emit_ep(1)
            else:
                for tc in range(2):
                    g = 2 * nt + tc
                    emit_k(g)
                    emit_fin2(g)
                    emit_ep(g)

        if jit:
            row = 96 if mode == "attn_jit" else 64
            xpm2 = wsum_pm if mode == "attn_jit" else bsum_pm
            ptr = pmain[3] if mode == "attn_jit" else pgate

            def f_tr(t):
                t.wait_ge(s_ws.h, ws_base + NT)
                for nt2 in range(NT):
                    li = t.matmul(ptr[:, nt2:nt2 + 1],
                                  aux_bf[row:row + 1, nt2 * 128:(nt2 + 1) * 128],
                                  ones[row:row + 1, 0:1], start=True, stop=True,
                                  tile_position=(row, 0), skip_group_check=True)
                s_ax.inc(li)
            block.tensor(f_tr)
            axtr = s_ax.n

            def f_tre(v):
                v.wait_ge(s_ax.h, axtr)
                li = v.tensor_copy(xpm2[:, 0:NT], ptr[:, 0:NT])
                s_bv.inc(li)
            block.vector(f_tre)

        if aux_next is not None:
            emit_aux_upto(KT)

    def f_p0_sync(sy):
        s_ld.inc(sy.dma_start(idsb[:], ident.ap()), 16)
        s_ld.inc(sy.dma_start(ones[:], ones_c.ap()), 16)
        s_ld.inc(sy.dma_start(bpksb[:], bpk.ap()), 16)
        s_ld.inc(sy.dma_start(biasA[:], biasA_d), 16)
        s_ld.inc(sy.dma_start(biasB_t[:], biasB_d), 16)
        f_mask_dma(sy, 19)
    block.sync(f_p0_sync)
    block.vector(lambda v: f_mask_red(v, 16))

    def f_p0_g(g):
        s_ldg.inc(g.dma_start(gw_s[:].rearrange("p (kt e) -> p kt e", e=8),
                              gwT_d.rearrange("(kt p) e -> p kt e", p=128)), 16)
        s_ldg.inc(g.dma_start(acat_s[:].rearrange("p (kt e) -> p kt e", e=128),
                              acatT_d.rearrange("(kt p) e -> p kt e", p=128)), 16)
        s_ldg.inc(g.dma_start(bcat_s[:], bcat2_d), 16)
        for q in range(16):
            src = xT_in[q * 2 * 128:(q + 1) * 2 * 128, 0:TC]
            src = src.rearrange("(kt p) t -> p kt t", p=128)
            dst = xA[:, q * 2 * TC:(q + 1) * 2 * TC].rearrange("p (kt t) -> p kt t", t=TC)
            sem = s_xq[q]
            sem.inc(g.dma_start(dst, src), 16)
            xq_ready[q] = (sem.h, sem.n)
    block.gpsimd(f_p0_g)
    ld_all = s_ld.n
    ldg_all = s_ldg.n

    def f_p0_v(v):
        v.memset(epsc[:], EPS)
        v.wait_ge(s_ld.h, ld_all)
        v.wait_ge(s_ldg.h, ldg_all)
        li = v.tensor_scalar(biasBh[:], biasB_t[:], 0.5, None, ALU.mult)
        s_bv.inc(li)
    block.vector(f_p0_v)

    def f_p0_t(t):
        t.wait_ge(s_ld.h, ld_all)
        t.wait_ge(s_ldg.h, ldg_all)
        for kt in range(KT):
            t.matmul(pmain[0][32:33, 0:128], ones[:], acat_s[:, kt * 128:(kt + 1) * 128],
                     start=(kt == 0), stop=(kt == KT - 1), tile_position=(0, 32), skip_group_check=True)
            li = t.matmul(pmain[1][32:33, 0:8], ones[:], gw_s[:, kt * 8:(kt + 1) * 8],
                     start=(kt == 0), stop=(kt == KT - 1), tile_position=(0, 32), skip_group_check=True)
        s_ax.inc(li)
    block.tensor(f_p0_t)
    ax_p0 = s_ax.n

    def f_p0_v2(v):
        v.wait_ge(s_ax.h, ax_p0)
        v.tensor_copy(aux_bf[32:33, 0:128], pmain[0][32:33, 0:128])
        v.tensor_copy(aux_bf[32:33, 128:136], pmain[1][32:33, 0:8])
        s_bv.incd(v)
    block.vector(f_p0_v2)

    for kt in range(KT):
        emit_aux_kt(kt, xA, True, *xq_ready[kt // 2])

    emit_sublayer("s1c0", "ffn", 0, xA, xB, ("ep", False), jit_bsum=True, mask_s1=True)
    emit_sublayer("s2c0", "attn_jit", 0, xB, xA, ("ep", True))
    emit_sublayer("s3c0", "ffn", 0, xA, None, ("dma", True, xB), x1_load=True)

    emit_sublayer("s1c1", "ffn", 1, xB, xA, ("ep", False))
    emit_sublayer("s2c1", "attn_scr", 1, xA, xB, ("ep", True))
    emit_sublayer("s3c1", "ffn", 1, xB, None, None)

    o0, o1 = s_out2[0].n, s_out2[1].n

    def f_tail(sy):
        sy.wait_ge(s_out2[0].h, o0)
        sy.wait_ge(s_out2[1].h, o1)
    block.sync(f_tail)

    ctx.close()
    return nc


def kernel(L0_attn_data, L0_attn_mask, L0_attn_weight, L0_attn_bias,
           L0_ffn_gate_w, L0_ffn_base_w, L0_ffn_base_b, L0_ffn_lora_A,
           L0_ffn_lora_B):
    data = np.asarray(L0_attn_data, dtype=np.float32).reshape(16384, D)
    maskT_h = np.ascontiguousarray(np.asarray(L0_attn_mask, dtype=np.int32).T)
    def colblocks(w):
        wt = np.asarray(w, dtype=np.float32).T.reshape(KT, 128, NT, 128)
        return np.ascontiguousarray(wt.transpose(2, 1, 0, 3)).reshape(NT, 128, KT * 128)

    wT_h = colblocks(L0_attn_weight)
    baseT_h = colblocks(L0_ffn_base_w)
    gwT_h = np.ascontiguousarray(np.asarray(L0_ffn_gate_w, dtype=np.float32).T)
    acatT_h = np.ascontiguousarray(
        np.asarray(L0_ffn_lora_A, dtype=np.float32).reshape(128, D).T)
    bcat2_h = np.ascontiguousarray(
        np.asarray(L0_ffn_lora_B, dtype=np.float32).transpose(0, 2, 1).reshape(128, D))
    biasA_h = np.ascontiguousarray(np.asarray(L0_attn_bias, dtype=np.float32).reshape(32, 128).T)
    biasB_h = np.ascontiguousarray(np.asarray(L0_ffn_base_b, dtype=np.float32).reshape(32, 128).T)

    nc = build_program()
    in_maps = []
    for c in range(NCORES):
        xT = np.ascontiguousarray(data[c * TPC:(c + 1) * TPC, :].T)
        in_maps.append({
            "xT": xT, "maskT": maskT_h, "wT": wT_h, "baseT": baseT_h, "gwT": gwT_h,
            "acatT": acatT_h, "bcat2": bcat2_h, "biasA": biasA_h, "biasB": biasB_h,
        })
    res = run_bass_kernel_spmd(nc, in_maps, core_ids=list(range(NCORES)))
    parts = [res.results[c]["outT"].T for c in range(NCORES)]
    out = np.concatenate(parts, axis=0).reshape(2, 8192, D).astype(np.float32)
    return out


# version 21
# speedup vs baseline: 1.0807x; 1.0050x over previous
import numpy as np
import ml_dtypes
from contextlib import ExitStack

import concourse.bass as bass
import concourse.mybir as mybir
from concourse.bass_utils import run_bass_kernel_spmd

F32 = mybir.dt.float32
BF16 = mybir.dt.bfloat16
I32 = mybir.dt.int32
AF = mybir.ActivationFunctionType
ALU = mybir.AluOpType
AX = mybir.AxisListType
bfdt = ml_dtypes.bfloat16

ROT4 = True
D = 4096
KT = 32
NT = 32
TPC = 2048
TC = 1024
NCORES = 8
INV_D = 1.0 / D
EPS = 1e-5


class Sem:
    def __init__(self, h):
        self.h = h
        self.n = 0

    def inc(self, inst, by=1):
        inst.then_inc(self.h, by)
        self.n += by
        return self.n

    def incd(self, eng, by=1):
        eng.drain()
        eng.sem_inc(self.h, by)
        self.n += by
        return self.n


def build_program():
    nc = bass.Bass("TRN2", target_bir_lowering=False)

    xT_in = nc.dram_tensor("xT", [D, TPC], F32, kind="ExternalInput").ap()
    maskT = nc.dram_tensor("maskT", [D, D], I32, kind="ExternalInput").ap()
    wT = nc.dram_tensor("wT", [NT, 128, KT * 128], F32, kind="ExternalInput").ap()
    baseT = nc.dram_tensor("baseT", [NT, 128, KT * 128], F32, kind="ExternalInput").ap()
    gwT_d = nc.dram_tensor("gwT", [D, 8], F32, kind="ExternalInput").ap()
    acatT_d = nc.dram_tensor("acatT", [D, 128], F32, kind="ExternalInput").ap()
    bcat2_d = nc.dram_tensor("bcat2", [128, D], F32, kind="ExternalInput").ap()
    biasA_d = nc.dram_tensor("biasA", [128, 32], F32, kind="ExternalInput").ap()
    biasB_d = nc.dram_tensor("biasB", [128, 32], F32, kind="ExternalInput").ap()
    outT = nc.dram_tensor("outT", [D, TPC], F32, kind="ExternalOutput").ap()
    scrA = nc.dram_tensor("scrA", [NT, 128, KT * 128], BF16, kind="Internal").ap()
    dscr = nc.dram_tensor("dscr", [16, 1024], F32, kind="Internal").ap()
    dbm = nc.dram_tensor("dbm", [128, 128], BF16, kind="Internal").ap()
    dbw = nc.dram_tensor("dbw", [8, TC], BF16, kind="Internal").ap()
    dbr = nc.dram_tensor("dbr", [2, 512], BF16, kind="Internal").ap()

    ident = nc.inline_tensor(np.eye(128, dtype=bfdt), "ident")
    ones_c = nc.inline_tensor(np.ones((128, 1), dtype=bfdt), "ones_c")
    bpk_np = np.zeros((128, 4), dtype=bfdt)
    for p in range(128):
        bpk_np[p, p // 32] = 1.0
    bpk = nc.inline_tensor(bpk_np, "bpk")

    ctx = ExitStack()
    sb = lambda nm, sh, dt: ctx.enter_context(nc.sbuf_tensor(nm, sh, dt))

    xA = sb("xA", [128, KT * TC], BF16)
    xB = sb("xB", [128, KT * TC], BF16)
    wcol = [sb(f"wcol{i}", [128, KT * 128], BF16) for i in range(2)]
    acat_s = sb("acat_s", [128, KT * 128], BF16)
    bcat_s = sb("bcat_s", [128, D], BF16)
    gw_s = sb("gw_s", [128, KT * 8], BF16)
    idsb = sb("idsb", [128, 128], BF16)
    ones = sb("ones", [128, 1], BF16)
    bpksb = sb("bpksb", [128, 4], BF16)
    biasA = sb("biasA_s", [128, 32], F32)
    biasBh = sb("biasBh", [128, 32], F32)
    biasB_t = sb("biasB_t", [128, 32], F32)
    aux_bf = sb("aux_bf", [128, 5120], BF16)
    rows_f = sb("rows_f", [128, 2048], F32)
    r_b = sb("r_b", [128, TC], F32)
    w2exp = sb("w2exp", [128, TC], BF16)
    rmb = sb("rmb", [128, TC], BF16)
    bsum_pm = sb("bsum_pm", [128, 32], F32)
    wsum_pm = sb("wsum_pm", [128, 32], F32)
    zs = sb("zs", [128, TC], BF16)
    sqb = [sb(f"sqb{i}", [128, TC], BF16) for i in range(2)]
    gstage = sb("gstage", [128, 512], BF16)
    epsc = sb("epsc", [128, 1], F32)
    ostg = [sb(f"ostg{i}", [128, 512], F32) for i in range(2)]
    kexp = sb("kexp", [128, KT * 128], BF16)

    pmain = [ctx.enter_context(nc.psum_tensor(f"pm{i}", [128, 512], F32)) for i in range(4)]
    pstat = ctx.enter_context(nc.psum_tensor("pstat", [128, 512], F32))
    pgate = ctx.enter_context(nc.psum_tensor("pgate", [128, 512], F32))
    plow = [ctx.enter_context(nc.psum_tensor(f"pl{i}", [128, 512], F32)) for i in range(2)]

    names = ["s_wcs", "s_bc", "s_scw", "s_out", "s_ld", "s_ldg",
             "s_mm", "s_ep", "s_sq", "s_ax", "s_wm", "s_ws", "s_m1", "s_m1e", "s_bv"]
    S = {nm: Sem(ctx.enter_context(nc.semaphore(nm))) for nm in names}
    s_wcs, s_bc, s_scw, s_out, s_ld, s_ldg = (S[n] for n in names[:6])
    s_mm, s_ep, s_sq, s_ax, s_wm, s_ws, s_m1, s_m1e, s_bv = (S[n] for n in names[6:])
    s_xq = [Sem(ctx.enter_context(nc.semaphore(f"s_xq{i}"))) for i in range(16)]
    s_wc2 = [Sem(ctx.enter_context(nc.semaphore(f"s_wc{i}"))) for i in range(2)]
    s_mk4 = [Sem(ctx.enter_context(nc.semaphore(f"s_mk{i}"))) for i in range(4)]
    s_cvm = Sem(ctx.enter_context(nc.semaphore("s_cvm")))
    s_wcs2 = [Sem(ctx.enter_context(nc.semaphore(f"s_wcs{i}"))) for i in range(2)]
    s_scw2 = [Sem(ctx.enter_context(nc.semaphore(f"s_scw{i}"))) for i in range(2)]
    s_out2 = [Sem(ctx.enter_context(nc.semaphore(f"s_out{i}"))) for i in range(2)]
    s_kx = Sem(ctx.enter_context(nc.semaphore("s_kx")))
    s_cs2 = [Sem(ctx.enter_context(nc.semaphore(f"s_cs{i}"))) for i in range(2)]
    s_mc = Sem(ctx.enter_context(nc.semaphore("s_mc")))
    xq_ready = [None] * 16
    s_bar = Sem(ctx.enter_context(nc.semaphore("s_bar")))
    bar_k = [0]

    def barrier():
        bar_k[0] += 1
        tgt = 5 * bar_k[0]

        def mk(eng):
            eng.drain()
            eng.sem_inc(s_bar.h, 1)
            eng.wait_ge(s_bar.h, tgt)
        block.tensor(mk)
        block.vector(mk)
        block.scalar(mk)
        block.gpsimd(mk)
        block.sync(mk)
        s_bar.n = tgt

    block = ctx.enter_context(nc.Block())

    mskraw = [kexp[:, i * 1024:(i + 1) * 1024].bitcast(I32) for i in range(3)]
    knb_f = kexp[:, 3072:3584].bitcast(F32)
    knb_b = kexp[:, 3584:3968]
    bm01T = kexp[:, 3968:4096]
    gv = rows_f[:, 512:576]
    wvrep = aux_bf[:, 0:1024]
    m8all = rows_f[:, 576:640]
    gtmp1 = rows_f[:, 640:704]
    gtmp2 = rows_f[:, 704:768]
    gtmp3 = rows_f[:, 768:832]
    wv = rows_f[:, 832:864].bitcast(BF16)
    bm01 = sb("bm01", [128, 128], BF16)

    kexp_holder = {"kexp": kexp}
    aux_guard = [0, 0]

    def x_tile(xbuf, kt, tc=None):
        if tc is None:
            return xbuf[:, kt * TC:(kt + 1) * TC]
        return xbuf[:, kt * TC + tc * 512: kt * TC + (tc + 1) * 512]

    def wcol_tile(buf, kt):
        return buf[:, kt * 128:(kt + 1) * 128]

    msk = {"dma": 0, "red": 0, "mm": 0}

    def f_mask_dma(sy, hi):
        while msk["dma"] < hi:
            s = msk["dma"]
            kt, ncn = divmod(s, 8)
            if s >= 3:
                sy.wait_ge(s_mc.h, s - 2)
            s_mk4[s % 3].inc(sy.dma_start(
                mskraw[s % 3],
                maskT[kt * 128:(kt + 1) * 128, ncn * 512:(ncn + 1) * 512]), 16)
            msk["dma"] += 1

    def f_mask_red(v, hi):
        while msk["red"] < hi:
            s = msk["red"]
            kt, ncn = divmod(s, 8)
            b = s % 3
            v.wait_ge(s_mk4[b].h, 16 * (s // 3 + 1))
            src = mskraw[b].rearrange("p (nb w) -> p nb w", w=32)
            s_mc.inc(v.reduce_sum(
                knb_f[:, (kt % 2) * 128 + ncn * 16: (kt % 2) * 128 + (ncn + 1) * 16],
                src, axis=AX.X))
            if ncn == 7:
                if kt >= 3:
                    v.wait_ge(s_m1.h, kt - 2)
                v.drain()
                s_cvm.inc(v.tensor_copy(
                    knb_b[:, (kt % 3) * 128:(kt % 3 + 1) * 128],
                    knb_f[:, (kt % 2) * 128:(kt % 2 + 1) * 128]))
            msk["red"] += 1

    def f_mask_mm(t, hi, bv_gate):
        while msk["mm"] < hi:
            kt = msk["mm"]
            if kt == 0:
                t.wait_ge(s_bv.h, bv_gate)
            t.wait_ge(s_cvm.h, kt + 1)
            s_m1.inc(t.matmul(plow[1][:, 4 * kt:4 * kt + 4],
                              knb_b[:, (kt % 3) * 128:(kt % 3 + 1) * 128],
                              bpksb[:], start=True, stop=True, skip_group_check=True))
            msk["mm"] += 1

    def emit_aux_kt(kt, xbuf, do_gl, src_sem_h, src_val, extra_bv=None):
        guard = aux_guard[kt % 2]
        sq = sqb[kt % 2]

        def f_sc(sc):
            sc.wait_ge(src_sem_h, src_val)
            if guard:
                sc.wait_ge(s_ax.h, guard)
            s_sq.inc(sc.activation(sq[:], x_tile(xbuf, kt), AF.Square))
        block.scalar(f_sc)
        sqv = s_sq.n

        def f_t(t):
            t.wait_ge(s_sq.h, sqv)
            if extra_bv is not None:
                t.wait_ge(s_bv.h, extra_bv)
            last = (kt == KT - 1)
            for tc in range(2):
                li = t.matmul(pstat[64 * tc:64 * tc + 1, :], ones[:], x_tile(xbuf, kt, tc),
                              start=(kt == 0), stop=last,
                              tile_position=(0, 64 * tc), skip_group_check=True)
                li = t.matmul(pstat[64 * tc + 32:64 * tc + 33, :], ones[:],
                              sq[:, tc * 512:(tc + 1) * 512],
                              start=(kt == 0), stop=last, tile_position=(0, 64 * tc + 32), skip_group_check=True)
                if do_gl:
                    li = t.matmul(pgate[32 * tc:32 * tc + 8, :], gw_s[:, kt * 8:(kt + 1) * 8],
                                  x_tile(xbuf, kt, tc), start=(kt == 0),
                                  stop=False, tile_position=(0, 32 * tc), skip_group_check=True)
                    li = t.matmul(plow[tc][:], acat_s[:, kt * 128:(kt + 1) * 128],
                             x_tile(xbuf, kt, tc), start=(kt == 0), stop=False, skip_group_check=True)
            s_ax.inc(li)
        block.tensor(f_t)
        aux_guard[kt % 2] = s_ax.n

    def emit_boundary_chain(is_ffn):
        def f_v(v):
            for tc, p in ((0, 0), (1, 64)):
                v.tensor_scalar(rows_f[p:p + 1, 0:512], pstat[p:p + 1, :], INV_D, None, ALU.mult)
                v.tensor_scalar(rows_f[p + 32:p + 33, 0:512], pstat[p + 32:p + 33, :], INV_D, None, ALU.mult)
            v.drain()
            for tc, p in ((0, 0), (1, 64)):
                v.tensor_scalar(aux_bf[p:p + 1, 2560:3072], rows_f[p:p + 1, 0:512], -1.0, None, ALU.mult)
            s_bv.incd(v)
        block.vector(f_v)
        bv_var = s_bv.n

        def f_hop(sc):
            sc.wait_ge(s_bv.h, bv_var)
            for tc, p in ((0, 0), (1, 64)):
                s_bc.inc(sc.dma_start(rows_f[p:p + 1, 512:1024], rows_f[p + 32:p + 33, 0:512]), 16)
            sc.wait_ge(s_bc.h, s_bc.n)
        block.scalar(f_hop)
        bc_hop = s_bc.n

        def f_v1b(v):
            v.wait_ge(s_bc.h, bc_hop)
            for tc, p in ((0, 0), (1, 64)):
                v.tensor_mul(rows_f[p:p + 1, 1024:1536], rows_f[p:p + 1, 0:512], rows_f[p:p + 1, 0:512])
            v.drain()
            for tc, p in ((0, 0), (1, 64)):
                v.tensor_sub(rows_f[p:p + 1, 512:1024], rows_f[p:p + 1, 512:1024],
                             rows_f[p:p + 1, 1024:1536])
            s_bv.incd(v)
        block.vector(f_v1b)
        bv_var2 = s_bv.n

        def f_sc(sc):
            sc.wait_ge(s_bv.h, bv_var2)
            for tc, p in ((0, 0), (1, 64)):
                sc.activation(rows_f[p:p + 1, 1024:1536], rows_f[p:p + 1, 512:1024],
                              AF.Sqrt, bias=epsc[p:p + 1, :])
            s_bv.incd(sc)
        block.scalar(f_sc)
        bv_sig = s_bv.n

        def f_v2(v):
            v.wait_ge(s_bv.h, bv_sig)
            for tc, p in ((0, 0), (1, 64)):
                v.reciprocal(rows_f[p:p + 1, 1536:2048], rows_f[p:p + 1, 1024:1536])
            v.drain()
            sc1 = 0.5 if is_ffn else 1.0
            for tc, p in ((0, 0), (1, 64)):
                v.scalar_tensor_tensor(aux_bf[p:p + 1, 3072:3584],
                                       rows_f[p:p + 1, 1536:2048], sc1,
                                       aux_bf[p:p + 1, 2560:3072], ALU.mult, ALU.mult)
            s_bv.incd(v)
        block.vector(f_v2)
        bv_r = s_bv.n

        def f_d2a(sc):
            sc.wait_ge(s_bv.h, bv_var)
            if is_ffn:
                s_bc.inc(sc.dma_start(aux_bf[32:33, 3584:4096], aux_bf[0:1, 2560:3072]), 16)
                s_bc.inc(sc.dma_start(aux_bf[32:33, 4096:4608], aux_bf[64:65, 2560:3072]), 16)
            row = 64 if is_ffn else 96
            s_bc.inc(sc.dma_start(aux_bf[row:row + 1, 4096:4608], aux_bf[0:1, 2560:3072]), 16)
            s_bc.inc(sc.dma_start(aux_bf[row:row + 1, 4608:5120], aux_bf[64:65, 2560:3072]), 16)
            sc.wait_ge(s_bc.h, s_bc.n)
        block.scalar(f_d2a)
        bc_msl = s_bc.n

        def f_d2(sc):
            sc.wait_ge(s_bv.h, bv_r)
            s_bc.inc(sc.dma_start(dscr[0:1, 0:512], rows_f[0:1, 1536:2048]), 16)
            s_bc.inc(sc.dma_start(dscr[1:2, 0:512], rows_f[64:65, 1536:2048]), 16)
            s_bc.inc(sc.dma_start(dbr[0:1, :], aux_bf[0:1, 3072:3584]), 16)
            s_bc.inc(sc.dma_start(dbr[1:2, :], aux_bf[64:65, 3072:3584]), 16)
            sc.wait_ge(s_bc.h, s_bc.n)
            s_bc.inc(sc.dma_start(r_b[:, 0:512], dscr[0:1, 0:512].partition_broadcast(128)), 16)
            s_bc.inc(sc.dma_start(r_b[:, 512:1024], dscr[1:2, 0:512].partition_broadcast(128)), 16)
            s_bc.inc(sc.dma_start(rmb[:, 0:512], dbr[0:1, :].partition_broadcast(128)), 16)
            s_bc.inc(sc.dma_start(rmb[:, 512:1024], dbr[1:2, :].partition_broadcast(128)), 16)
            sc.wait_ge(s_bc.h, s_bc.n)
        block.scalar(f_d2)
        bc_done = s_bc.n

        if not is_ffn:
            return {"bc_done": bc_done, "bc_msl": bc_msl, "bv_end": bv_r, "bv_var": bv_var}

        def f_t(t):
            t.wait_ge(s_bc.h, bc_msl)
            for tc in range(2):
                msl = aux_bf[32:33, 3584 + tc * 512: 3584 + (tc + 1) * 512]
                t.matmul(pgate[32 * tc:32 * tc + 8, :], aux_bf[32:33, 128:136], msl,
                         start=False, stop=True, tile_position=(32, 32 * tc), skip_group_check=True)
                li = t.matmul(plow[tc][:], aux_bf[32:33, 0:128], msl,
                              start=False, stop=True, tile_position=(32, 0), skip_group_check=True)
            s_ax.inc(li)
        block.tensor(f_t)
        ax_mr = s_ax.n

        def f_v3(v):
            v.wait_ge(s_ax.h, ax_mr)
            v.wait_ge(s_bc.h, bc_done)
            v.tensor_mul(gstage[0:8, 0:512], pgate[0:8, :], r_b[0:8, 0:512])
            v.tensor_mul(gstage[32:40, 0:512], pgate[32:40, :], r_b[32:40, 512:1024])
            s_bv.incd(v)
        block.vector(f_v3)
        bv_gl = s_bv.n

        def f_t2(t):
            t.wait_ge(s_bv.h, bv_gl)
            for tt in range(8):
                half = tt // 4
                lhs = gstage[32 * half:32 * half + 8, (tt % 4) * 128:(tt % 4 + 1) * 128]
                rhs = idsb[32 * half:32 * half + 8, 32 * half:32 * half + 8]
                li = t.matmul(pstat[:, tt * 8:(tt + 1) * 8], lhs, rhs,
                              start=True, stop=True, tile_position=(32 * half, 0),
                              skip_group_check=True)
            s_ax.inc(li)
        block.tensor(f_t2)
        ax_tr = s_ax.n

        def f_v4(v):
            v.wait_ge(s_ax.h, ax_tr)
            v.tensor_copy(gv[:], pstat[:, 0:64])
            v.drain()
            for s in range(8):
                v.max(m8all[:, s * 8:(s + 1) * 8], gv[:, s * 8:(s + 1) * 8])
            v.drain()
            m3 = m8all[:].rearrange("p (s e) -> p s e", e=8)
            v.tensor_copy(gtmp1[:].rearrange("p (s e) -> p s e", e=8),
                          m3[:, :, 1:2].broadcast_to([128, 8, 8]))
            v.tensor_copy(gtmp2[:].rearrange("p (s e) -> p s e", e=8),
                          m3[:, :, 0:1].broadcast_to([128, 8, 8]))
            v.drain()
            v.tensor_tensor(gtmp3[:], gv[:], gtmp1[:], ALU.is_ge)
            v.tensor_sub(gtmp2[:], gv[:], gtmp2[:])
            s_bv.incd(v)
        block.vector(f_v4)
        bv_g1 = s_bv.n

        def f_sc2(sc):
            sc.wait_ge(s_bv.h, bv_g1)
            li = sc.activation(gtmp2[:], gtmp2[:], AF.Exp)
            s_bv.inc(li)
        block.scalar(f_sc2)
        bv_g2 = s_bv.n

        def f_v5(v):
            v.wait_ge(s_bv.h, bv_g2)
            v.tensor_mul(gtmp1[:], gtmp2[:], gtmp3[:])
            v.drain()
            v.reduce_sum(gtmp2[:, 0:8], gtmp1[:].rearrange("p (s e) -> p s e", e=8), axis=AX.X)
            v.drain()
            v.reciprocal(gtmp2[:, 8:16], gtmp2[:, 0:8])
            v.drain()
            rdex = gtmp2[:, 8:16].rearrange("p (s o) -> p s o", o=1).broadcast_to([128, 8, 8])
            v.tensor_copy(gtmp3[:].rearrange("p (s e) -> p s e", e=8), rdex)
            v.drain()
            v.scalar_tensor_tensor(wv[:], gtmp1[:], 2.0, gtmp3[:], ALU.mult, ALU.mult)
            s_bv.incd(v)
        block.vector(f_v5)
        bv_g3 = s_bv.n

        def f_t3(t):
            t.wait_ge(s_bv.h, bv_g3)
            for tt in range(8):
                row = 0 if tt < 4 else 32
                li = t.matmul(pgate[row:row + 8, (tt % 4) * 128:(tt % 4 + 1) * 128],
                              wv[:, tt * 8:(tt + 1) * 8], idsb[:, 0:128],
                              start=True, stop=True, tile_position=(0, row),
                              skip_group_check=True)
            s_ax.inc(li)
        block.tensor(f_t3)
        ax_wt = s_ax.n

        def f_v6(v):
            v.wait_ge(s_ax.h, ax_wt)
            v.tensor_copy(aux_bf[0:8, 0:512], pgate[0:8, :])
            v.drain()
            v.tensor_copy(aux_bf[0:8, 512:1024], pgate[32:40, :])
            s_bv.incd(v)
        block.vector(f_v6)
        bv_wf = s_bv.n

        def f_d3(sc):
            sc.wait_ge(s_bv.h, bv_wf)
            s_bc.inc(sc.dma_start(dbw, aux_bf[0:8, 0:1024]), 16)
            sc.wait_ge(s_bc.h, s_bc.n)
            for e in range(8):
                s_bc.inc(sc.dma_start(w2exp[e * 16:(e + 1) * 16, :],
                                      dbw[e:e + 1, :].partition_broadcast(16)), 16)
            sc.wait_ge(s_bc.h, s_bc.n)
        block.scalar(f_d3)
        bc_w2 = s_bc.n

        def f_v7(v):
            v.wait_ge(s_bc.h, bc_w2)
            for tc in range(2):
                v.tensor_mul(zs[:, tc * 512:(tc + 1) * 512], plow[tc][:],
                             w2exp[:, tc * 512:(tc + 1) * 512])
            s_bv.incd(v)
        block.vector(f_v7)
        return {"bc_done": bc_done, "bc_msl": bc_msl, "bv_end": s_bv.n, "bv_var": bv_var}

    def emit_sublayer(name, mode, chunk, xsrc, xdst, aux_next, jit_bsum=False,
                      mask_s1=False, x1_load=False, skip_boundary=False):
        is_ffn = (mode == "ffn")
        col_ready = {}
        prev_mm = s_mm.n
        ep_base = s_ep.n
        mm_base = s_mm.n
        wm_base = s_wm.n
        ws_base = s_ws.n
        scw_base = [s_scw2[0].n, s_scw2[1].n]
        out_base = [s_out2[0].n, s_out2[1].n]
        scw_tot = (s_scw2[0].n, s_scw2[1].n)

        cs_ax = {}

        def issue_col_dma(eng, nt):
            if nt < 2:
                eng.wait_ge(s_mm.h, prev_mm - 2 + 2 * nt if prev_mm >= 2 else prev_mm)
                eng.wait_ge(s_scw2[0].h, scw_tot[0])
                eng.wait_ge(s_scw2[1].h, scw_tot[1])
            if nt >= 2:
                eng.wait_ge(s_mm.h, mm_base + 2 * (nt - 1))
                if mode == "attn_jit":
                    eng.wait_ge(s_scw2[nt % 2].h, scw_base[nt % 2] + 16 * (nt // 2))
            dst = wcol[nt % 2]
            if mode == "ffn":
                src = baseT[nt]
            elif mode == "attn_jit":
                src = wT[nt]
            else:
                src = scrA[nt]
            wsem = s_wcs2[nt % 2] if mode == "attn_scr" else s_wc2[nt % 2]
            wsem.inc(eng.dma_start(dst[0:64, :], src[0:64]), 16)
            wsem.inc(eng.dma_start(dst[64:128, :], src[64:128]), 16)
            col_ready[nt] = (wsem.h, wsem.n)

        col_q = block.sync if mode == "attn_scr" else block.gpsimd

        def f_cols01(eng):
            issue_col_dma(eng, 0)
            issue_col_dma(eng, 1)
        col_q(f_cols01)
        barrier()

        jit = (mode == "attn_jit" or jit_bsum)
        rot = 3 if (mode == "attn_jit" or not ROT4) else 4

        def emit_k(g):
            nt, tc = g // 2, g % 2
            pm = pmain[g % rot]
            plain_attn = (not is_ffn and not jit)

            def f_k(t):
                if g < rot:
                    t.wait_ge(s_ep.h, ep_base)
                else:
                    t.wait_ge(s_ep.h, ep_base + g - (rot - 1))
                if mode == "attn_jit":
                    t.wait_ge(s_wm.h, wm_base + nt + 1)
                else:
                    t.wait_ge(*col_ready[nt])
                li = None
                for kt in range(KT):
                    li = t.matmul(pm[:], wcol_tile(wcol[nt % 2], kt), x_tile(xsrc, kt, tc),
                                  start=(kt == 0), stop=(kt == KT - 1 and plain_attn),
                                  skip_group_check=True)
                if plain_attn:
                    s_mm.inc(li)
            block.tensor(f_k)

        def emit_fin2(g, bnd=None):
            if not is_ffn and not jit:
                return
            nt, tc = g // 2, g % 2
            pm = pmain[g % rot]

            def f_fin(t):
                if bnd is not None:
                    if jit:
                        t.wait_ge(s_bc.h, bnd["bc_msl"])
                    if is_ffn:
                        t.wait_ge(s_bv.h, bnd["bv_end"])
                li = None
                if jit:
                    t.wait_ge(s_ws.h, ws_base + nt + 1)
                    mrow = 64 if is_ffn else 96
                    li = t.matmul(pm[:], aux_bf[mrow:mrow + 1, nt * 128:(nt + 1) * 128],
                                  aux_bf[mrow:mrow + 1, 4096 + tc * 512:4096 + (tc + 1) * 512],
                                  start=False, stop=(not is_ffn), tile_position=(mrow, 0),
                                  skip_group_check=True)
                if is_ffn:
                    li = t.matmul(pm[:], bcat_s[:, nt * 128:(nt + 1) * 128],
                                  zs[:, tc * 512:(tc + 1) * 512], start=False, stop=True,
                                  skip_group_check=True)
                s_mm.inc(li)
            block.tensor(f_fin)

        def emit_ep(g):
            nt, tc = g // 2, g % 2
            pm = pmain[g % rot]
            mmv = s_mm.n

            xpm = (wsum_pm if mode.startswith("attn") else bsum_pm)

            def f_ep(v):
                v.wait_ge(s_mm.h, mmv)
                if g == 0:
                    v.wait_ge(s_bc.h, bnd["bc_done"])
                sc1 = 0.5 if is_ffn else 1.0
                v.scalar_tensor_tensor(pm[:], pm[:], sc1, r_b[:, tc * 512:(tc + 1) * 512],
                                       ALU.mult, ALU.mult)
                v.drain()
                if not jit:
                    v.scalar_tensor_tensor(pm[:], rmb[:, tc * 512:(tc + 1) * 512],
                                           xpm[:, nt:nt + 1], pm[:], ALU.mult, ALU.add)
                    v.drain()
                bias_ap = (biasBh if is_ffn else biasA)[:, nt:nt + 1]
                if xdst is not None:
                    li = v.scalar_tensor_tensor(x_tile(xdst, nt, tc), pm[:], bias_ap,
                                                x_tile(xsrc, nt, tc), ALU.add, ALU.add)
                else:
                    v.wait_ge(s_out2[g % 2].h, out_base[g % 2] + 16 * (g // 2))
                    li = v.scalar_tensor_tensor(ostg[g % 2][:], pm[:], bias_ap,
                                                x_tile(xsrc, nt, tc), ALU.add, ALU.add)
                s_ep.inc(li)
            block.vector(f_ep)
            epv = s_ep.n

            if xdst is None:
                def f_out(sy):
                    sy.wait_ge(s_ep.h, epv)
                    dst = outT[nt * 128:(nt + 1) * 128,
                               chunk * TC + tc * 512: chunk * TC + (tc + 1) * 512]
                    s_out2[g % 2].inc(sy.dma_start(dst, ostg[g % 2][:]), 16)
                block.sync(f_out)

        if mode != "attn_jit":
            for g0 in range(2 if is_ffn else 0):
                emit_k(g0)

        bnd = emit_boundary_chain(is_ffn)

        aux_done = [0]

        def emit_aux_upto(upto):
            while aux_done[0] < upto:
                kt = aux_done[0]
                extra = bnd["bv_end"] if kt == 0 else None
                if aux_next[0] == "ep":
                    emit_aux_kt(kt, xdst, aux_next[1], s_ep.h, ep_base + 2 * kt + 2, extra)
                else:
                    sh, sv = xq_ready[kt // 4]
                    emit_aux_kt(kt, aux_next[2], aux_next[1], sh, sv, extra)
                aux_done[0] += 1

        for nt in range(NT):
            if x1_load and nt in (0, 2, 4, 6, 8, 10, 12, 14):
                q = nt // 2

                def f_x1(g, q=q):
                    src = xT_in[q * 4 * 128:(q + 1) * 4 * 128, TC:2 * TC]
                    src = src.rearrange("(kt p) t -> p kt t", p=128)
                    dst = xB[:, q * 4 * TC:(q + 1) * 4 * TC].rearrange("p (kt t) -> p kt t", t=TC)
                    sem = s_xq[q]
                    sem.inc(g.dma_start(dst, src), 16)
                    xq_ready[q] = (sem.h, sem.n)
                block.gpsimd(f_x1)

            if mode == "attn_jit":
                def f_vm(v, nt=nt):
                    v.wait_ge(*col_ready[nt])
                    if nt == 0:
                        v.wait_ge(s_kx.h, kexp_holder["ready"])
                    kexp = kexp_holder["kexp"]
                    src1 = kexp[:].rearrange("p (kt nb) -> p kt nb", nb=128)[:, :, 4 * nt:4 * nt + 4]
                    src1 = src1.rearrange("p kt (nb o) -> p kt nb o", o=1).broadcast_to([128, KT, 4, 32])
                    dst = wcol[nt % 2][:].rearrange("p (kt nb r) -> p kt nb r", nb=4, r=32)
                    li = v.tensor_tensor(dst, dst, src1, ALU.mult)
                    s_wm.inc(li)
                block.vector(f_vm)

                def f_scw(sy, nt=nt):
                    sy.wait_ge(s_wm.h, wm_base + nt + 1)
                    s_scw2[nt % 2].inc(sy.dma_start(scrA[nt], wcol[nt % 2][:]), 16)
                block.sync(f_scw)

            if mode == "attn_jit" or jit_bsum:
                row = 96 if mode == "attn_jit" else 64
                pcs = pmain[3] if mode == "attn_jit" else plow[0]
                xpm = wsum_pm if mode == "attn_jit" else bsum_pm

                pcs2 = pcs if nt % 2 == 0 else pgate

                def f_cs(t, nt=nt, row=row, pcs2=pcs2):
                    if nt == 0:
                        t.wait_ge(s_bv.h, bnd["bv_end"])
                    if mode == "attn_jit":
                        t.wait_ge(s_wm.h, wm_base + nt + 1)
                    else:
                        t.wait_ge(*col_ready[nt])
                    if nt > 1:
                        t.wait_ge(s_ws.h, ws_base + nt - 1)
                    for j in range(8):
                        li = t.matmul(pcs2[row:row + 1, :], ones[:],
                                      wcol[nt % 2][:, j * 512:(j + 1) * 512],
                                      start=(j == 0), stop=(j == 7),
                                      tile_position=(0, row), skip_group_check=True)
                    s_ax.inc(li)
                block.tensor(f_cs)
                ax_cs = s_ax.n
                cs_ax[nt] = ax_cs

                def f_csv(v, nt=nt, row=row, pcs2=pcs2):
                    v.wait_ge(s_ax.h, ax_cs)
                    src = pcs2[row:row + 1, :].rearrange("p (a n) -> p n a", a=4)
                    v.reduce_sum(rows_f[row:row + 1, 0:128], src, axis=AX.X)
                    v.drain()
                    li = v.tensor_copy(aux_bf[row:row + 1, nt * 128:(nt + 1) * 128],
                                       rows_f[row:row + 1, 0:128])
                    s_ws.inc(li)
                block.vector(f_csv)

            if mask_s1:
                red_hi = min(256, 16 + ((nt + 1) * 240 + 29) // 30)
                dma_hi = min(256, red_hi + 8)
                mm_hi = min(32, max(0, red_hi // 8 - 2))
                block.sync(lambda sy, h=dma_hi: f_mask_dma(sy, h))
                block.vector(lambda v, h=red_hi: f_mask_red(v, h))
                block.tensor(lambda t, h=mm_hi: f_mask_mm(t, h, bnd["bv_end"]))
                if nt == 29:
                    block.tensor(lambda t: f_mask_mm(t, 32, bnd["bv_end"]))

                    def f_bm_thr(v):
                        v.wait_ge(s_m1.h, 32)
                        s_bv.inc(v.tensor_scalar(bm01T[:], plow[1][:, 0:128], 0.5,
                                                 None, ALU.is_gt))
                    block.vector(f_bm_thr)
                    bv_thr = s_bv.n

                    def f_bm_tr(t):
                        t.wait_ge(s_bv.h, bv_thr)
                        s_m1.inc(t.matmul(plow[1][:, 128:256], bm01T[:], idsb[:, 0:128],
                                          start=True, stop=True, skip_group_check=True))
                    block.tensor(f_bm_tr)

                    def f_bm_ev(v):
                        v.wait_ge(s_m1.h, 33)
                        s_bv.inc(v.tensor_copy(bm01[:], plow[1][:, 128:256]))
                    block.vector(f_bm_ev)
                    bv_bm = s_bv.n

                    def f_kexp(sy):
                        sy.wait_ge(s_bv.h, bv_bm)
                        s_kx.inc(sy.dma_start(dbm, bm01[:]), 16)
                        sy.wait_ge(s_kx.h, s_kx.n)
                        kexp = kexp_holder["kexp"]
                        dbmv = dbm.rearrange("(kt j) n -> j kt n", j=4)
                        for j in range(4):
                            src = dbmv[j:j + 1].partition_broadcast(32)
                            s_kx.inc(sy.dma_start(kexp[32 * j:32 * j + 32, :], src), 16)
                        sy.wait_ge(s_kx.h, s_kx.n)
                    block.sync(f_kexp)
                    kexp_holder["ready"] = s_kx.n

            if nt + 2 < NT:
                col_q(lambda eng, n=nt + 2: issue_col_dma(eng, n))

            if aux_next is not None and nt >= 3:
                emit_aux_upto(nt - 2)

            if mode == "attn_jit" or not ROT4:
                if nt == 0:
                    if not is_ffn and mode == "attn_jit":
                        emit_k(0)
                        emit_fin2(0, bnd)
                        emit_ep(0)
                        emit_k(1)
                        emit_fin2(1, bnd)
                        emit_ep(1)
                    elif is_ffn:
                        emit_fin2(0, bnd)
                        emit_ep(0)
                        emit_fin2(1, bnd)
                        emit_ep(1)
                    else:
                        emit_k(0)
                        emit_fin2(0, bnd)
                        emit_ep(0)
                        emit_k(1)
                        emit_fin2(1, bnd)
                        emit_ep(1)
                else:
                    for tc in range(2):
                        g = 2 * nt + tc
                        emit_k(g)
                        emit_fin2(g)
                        emit_ep(g)
            elif nt == 0:
                if is_ffn:
                    emit_fin2(0, bnd)
                    emit_ep(0)
                    emit_fin2(1, bnd)
                    emit_ep(1)
                else:
                    emit_k(0)
                    emit_fin2(0, bnd)
                    emit_ep(0)
                    emit_k(1)
                    emit_fin2(1, bnd)
                    emit_ep(1)
            else:
                for tc in range(2):
                    g = 2 * nt + tc
                    emit_k(g)
                    emit_fin2(g)
                    emit_ep(g)

        if jit:
            row = 96 if mode == "attn_jit" else 64
            xpm2 = wsum_pm if mode == "attn_jit" else bsum_pm
            ptr = pmain[3] if mode == "attn_jit" else pgate

            def f_tr(t):
                t.wait_ge(s_ws.h, ws_base + NT)
                for nt2 in range(NT):
                    li = t.matmul(ptr[:, nt2:nt2 + 1],
                                  aux_bf[row:row + 1, nt2 * 128:(nt2 + 1) * 128],
                                  ones[row:row + 1, 0:1], start=True, stop=True,
                                  tile_position=(row, 0), skip_group_check=True)
                s_ax.inc(li)
            block.tensor(f_tr)
            axtr = s_ax.n

            def f_tre(v):
                v.wait_ge(s_ax.h, axtr)
                li = v.tensor_copy(xpm2[:, 0:NT], ptr[:, 0:NT])
                s_bv.inc(li)
            block.vector(f_tre)

        if aux_next is not None:
            emit_aux_upto(KT)

    def f_p0_sync(sy):
        s_ld.inc(sy.dma_start(idsb[:], ident.ap()), 16)
        s_ld.inc(sy.dma_start(ones[:], ones_c.ap()), 16)
        s_ld.inc(sy.dma_start(bpksb[:], bpk.ap()), 16)
        s_ld.inc(sy.dma_start(biasA[:], biasA_d), 16)
        s_ld.inc(sy.dma_start(biasB_t[:], biasB_d), 16)
        f_mask_dma(sy, 19)
    block.sync(f_p0_sync)
    block.vector(lambda v: f_mask_red(v, 16))

    def f_p0_g(g):
        s_ldg.inc(g.dma_start(gw_s[:].rearrange("p (kt e) -> p kt e", e=8),
                              gwT_d.rearrange("(kt p) e -> p kt e", p=128)), 16)
        s_ldg.inc(g.dma_start(acat_s[:].rearrange("p (kt e) -> p kt e", e=128),
                              acatT_d.rearrange("(kt p) e -> p kt e", p=128)), 16)
        s_ldg.inc(g.dma_start(bcat_s[:], bcat2_d), 16)
        for q in range(16):
            src = xT_in[q * 2 * 128:(q + 1) * 2 * 128, 0:TC]
            src = src.rearrange("(kt p) t -> p kt t", p=128)
            dst = xA[:, q * 2 * TC:(q + 1) * 2 * TC].rearrange("p (kt t) -> p kt t", t=TC)
            sem = s_xq[q]
            sem.inc(g.dma_start(dst, src), 16)
            xq_ready[q] = (sem.h, sem.n)
    block.gpsimd(f_p0_g)
    ld_all = s_ld.n
    ldg_all = s_ldg.n

    def f_p0_v(v):
        v.memset(epsc[:], EPS)
        v.wait_ge(s_ld.h, ld_all)
        v.wait_ge(s_ldg.h, ldg_all)
        li = v.tensor_scalar(biasBh[:], biasB_t[:], 0.5, None, ALU.mult)
        s_bv.inc(li)
    block.vector(f_p0_v)

    def f_p0_t(t):
        t.wait_ge(s_ld.h, ld_all)
        t.wait_ge(s_ldg.h, ldg_all)
        for kt in range(KT):
            t.matmul(pmain[0][32:33, 0:128], ones[:], acat_s[:, kt * 128:(kt + 1) * 128],
                     start=(kt == 0), stop=(kt == KT - 1), tile_position=(0, 32), skip_group_check=True)
            li = t.matmul(pmain[1][32:33, 0:8], ones[:], gw_s[:, kt * 8:(kt + 1) * 8],
                     start=(kt == 0), stop=(kt == KT - 1), tile_position=(0, 32), skip_group_check=True)
        s_ax.inc(li)
    block.tensor(f_p0_t)
    ax_p0 = s_ax.n

    def f_p0_v2(v):
        v.wait_ge(s_ax.h, ax_p0)
        v.tensor_copy(aux_bf[32:33, 0:128], pmain[0][32:33, 0:128])
        v.tensor_copy(aux_bf[32:33, 128:136], pmain[1][32:33, 0:8])
        s_bv.incd(v)
    block.vector(f_p0_v2)

    for kt in range(KT):
        emit_aux_kt(kt, xA, True, *xq_ready[kt // 2])

    emit_sublayer("s1c0", "ffn", 0, xA, xB, ("ep", False), jit_bsum=True, mask_s1=True)
    emit_sublayer("s2c0", "attn_jit", 0, xB, xA, ("ep", True))
    emit_sublayer("s3c0", "ffn", 0, xA, None, ("dma", True, xB), x1_load=True)

    emit_sublayer("s1c1", "ffn", 1, xB, xA, ("ep", False))
    emit_sublayer("s2c1", "attn_scr", 1, xA, xB, ("ep", True))
    emit_sublayer("s3c1", "ffn", 1, xB, None, None)

    o0, o1 = s_out2[0].n, s_out2[1].n

    def f_tail(sy):
        sy.wait_ge(s_out2[0].h, o0)
        sy.wait_ge(s_out2[1].h, o1)
    block.sync(f_tail)

    ctx.close()
    return nc


def kernel(L0_attn_data, L0_attn_mask, L0_attn_weight, L0_attn_bias,
           L0_ffn_gate_w, L0_ffn_base_w, L0_ffn_base_b, L0_ffn_lora_A,
           L0_ffn_lora_B):
    data = np.asarray(L0_attn_data, dtype=np.float32).reshape(16384, D)
    maskT_h = np.ascontiguousarray(np.asarray(L0_attn_mask, dtype=np.int32).T)
    def colblocks(w):
        wt = np.asarray(w, dtype=np.float32).T.reshape(KT, 128, NT, 128)
        return np.ascontiguousarray(wt.transpose(2, 1, 0, 3)).reshape(NT, 128, KT * 128)

    wT_h = colblocks(L0_attn_weight)
    baseT_h = colblocks(L0_ffn_base_w)
    gwT_h = np.ascontiguousarray(np.asarray(L0_ffn_gate_w, dtype=np.float32).T)
    acatT_h = np.ascontiguousarray(
        np.asarray(L0_ffn_lora_A, dtype=np.float32).reshape(128, D).T)
    bcat2_h = np.ascontiguousarray(
        np.asarray(L0_ffn_lora_B, dtype=np.float32).transpose(0, 2, 1).reshape(128, D))
    biasA_h = np.ascontiguousarray(np.asarray(L0_attn_bias, dtype=np.float32).reshape(32, 128).T)
    biasB_h = np.ascontiguousarray(np.asarray(L0_ffn_base_b, dtype=np.float32).reshape(32, 128).T)

    nc = build_program()
    in_maps = []
    for c in range(NCORES):
        xT = np.ascontiguousarray(data[c * TPC:(c + 1) * TPC, :].T)
        in_maps.append({
            "xT": xT, "maskT": maskT_h, "wT": wT_h, "baseT": baseT_h, "gwT": gwT_h,
            "acatT": acatT_h, "bcat2": bcat2_h, "biasA": biasA_h, "biasB": biasB_h,
        })
    res = run_bass_kernel_spmd(nc, in_maps, core_ids=list(range(NCORES)))
    parts = [res.results[c]["outT"].T for c in range(NCORES)]
    out = np.concatenate(parts, axis=0).reshape(2, 8192, D).astype(np.float32)
    return out
